# version 20
# baseline (speedup 1.0000x reference)
import contextlib
import numpy as np
import ml_dtypes
import concourse.bass as bass
import concourse.mybir as mybir
from concourse.bass_utils import run_bass_kernel_spmd

F32 = mybir.dt.float32
BF16 = mybir.dt.bfloat16
I32 = mybir.dt.int32
AF = mybir.ActivationFunctionType
ALU = mybir.AluOpType
AX = mybir.AxisListType

P = 128
T = 2048
E = 2048
DKV = 512
HQ, HKV, D = 32, 8, 64
NT = 16
NB = 4
EPS = 1e-5

CHUNKS = {0: (0, 3), 1: (1, 2)}
NKV = {0: 16, 1: 12}

SEM_NAMES = [
    "sFlag", "sX", "sW", "sLN", "sMean", "sXP", "sTm", "sKVe", "sRope",
    "sQT", "sB", "sS", "sE", "sM", "sAV", "sRC", "sBC", "sRB", "sYT",
    "sOv", "sOutW", "sOutF", "sSi", "sGa", "sDe", "sVs", "sCast",
]


class Sem:
    def __init__(self, h):
        self.h = h
        self.n = 0


class Emitter:
    ENGS = ("sync", "tensor", "vector", "scalar", "gpsimd")

    def __init__(self, nc):
        self.nc = nc
        self.pre = {e: [] for e in self.ENGS}
        self.bod = [{e: [] for e in self.ENGS}, {e: [] for e in self.ENGS}]
        self.cur = None

    def _q(self, eng):
        return self.pre[eng] if self.cur is None else self.bod[self.cur][eng]

    def op(self, eng, meth, *args, sem=None, k=1, **kw):
        nc = self.nc

        def fn(eng=eng, meth=meth, args=args, kw=kw, sem=sem, k=k):
            ins = getattr(getattr(nc, eng), meth)(*args, **kw)
            if sem is not None:
                ins.then_inc(sem.h, k)
        if sem is not None:
            sem.n += k
        self._q(eng).append(fn)
        return sem.n if sem is not None else None

    def wait(self, eng, sem, val=None):
        nc = self.nc
        v = sem.n if val is None else val
        if v <= 0:
            return

        def fn(eng=eng, sem=sem, v=v):
            getattr(nc, eng).wait_ge(sem.h, v)
        self._q(eng).append(fn)


def build_nc(debug=False, ffn_split=False):
    nc = bass.Bass("TRN2", target_bir_lowering=False)

    dx = nc.dram_tensor("x", [T, E], F32, kind="ExternalInput").ap()
    dwq = nc.dram_tensor("wq", [E, E], F32, kind="ExternalInput").ap()
    dwk = nc.dram_tensor("wk", [E, DKV], F32, kind="ExternalInput").ap()
    dwv = nc.dram_tensor("wv", [E, DKV], F32, kind="ExternalInput").ap()
    dwo = nc.dram_tensor("wo", [E, E], F32, kind="ExternalInput").ap()
    dbq = nc.dram_tensor("bq", [E], F32, kind="ExternalInput").ap()
    dbk = nc.dram_tensor("bk", [DKV], F32, kind="ExternalInput").ap()
    dbv = nc.dram_tensor("bv", [DKV], F32, kind="ExternalInput").ap()
    dbo = nc.dram_tensor("bo", [E], F32, kind="ExternalInput").ap()
    dwg = nc.dram_tensor("wg2", [32, 128, 4096], F32, kind="ExternalInput").ap()
    dwu = nc.dram_tensor("wu2", [32, 128, 4096], F32, kind="ExternalInput").ap()
    dwd = nc.dram_tensor("wd4", [4, 8, 128, 4096], F32, kind="ExternalInput").ap()
    dtq = nc.dram_tensor("tq", [T, 1024], F32, kind="ExternalInput").ap()
    dtk = nc.dram_tensor("tk", [T, 1024], F32, kind="ExternalInput").ap()
    dident = nc.dram_tensor("identm", [P, P], F32, kind="ExternalInput").ap()
    dmask = nc.dram_tensor("maskm", [P, P], F32, kind="ExternalInput").ap()
    dflag = nc.dram_tensor("flag", [1], I32, kind="ExternalInput").ap()
    dout = nc.dram_tensor("out", [1024, E], F32, kind="ExternalOutput").ap()
    ddbg = {}
    if debug:
        for nm, w in (("xnT", 16384), ("kaug", 16384), ("vall", 8320),
                      ("qaug", 16384), ("yt", 8192), ("misc", 512)):
            ddbg[nm] = nc.dram_tensor("dbg_" + nm, [P, w], F32, kind="ExternalOutput").ap()
        ddbg["qaug2"] = nc.dram_tensor("dbg_qaug2", [2, P, 16384], F32, kind="ExternalOutput").ap()
    dffn = None
    if ffn_split:
        dffn = nc.dram_tensor("dbg_ffn", [1024, E], F32, kind="ExternalOutput").ap()

    POOLW = 29824

    with contextlib.ExitStack() as stack:
        en = stack.enter_context
        pool = en(nc.sbuf_tensor("pool", [P, POOLW], F32))
        xnT_ = en(nc.sbuf_tensor("xnTw", [P, 8192], F32))
        tab_ = en(nc.sbuf_tensor("tabw", [P, 2048], F32))
        biasb = en(nc.sbuf_tensor("biasb", [P, 2048], F32))
        sg = en(nc.sbuf_tensor("sg", [P, 3072], F32))
        ropebf_ = en(nc.sbuf_tensor("ropew", [P, 512], F32))
        sqbf_ = en(nc.sbuf_tensor("sqw", [P, 256], F32))
        kropebf_ = en(nc.sbuf_tensor("kropew", [P, 256], F32))
        bob_ = en(nc.sbuf_tensor("bobw", [1, 1024], F32))
        rbbf = en(nc.sbuf_tensor("rbbf", [1, 1024], F32))
        rbf2 = en(nc.sbuf_tensor("rbf2", [1, 2048], F32))
        fl = en(nc.sbuf_tensor("fl", [1, 1], I32))
        ident_ = en(nc.sbuf_tensor("identw", [P, 64], F32))
        identf = en(nc.sbuf_tensor("identf", [P, 128], F32))
        mask_ = en(nc.sbuf_tensor("maskw", [P, 64], F32))
        onesb_ = en(nc.sbuf_tensor("onesbw", [1, 64], F32))
        onesf = en(nc.sbuf_tensor("onesf", [1, 128], F32))
        km32 = en(nc.sbuf_tensor("km32", [1, 32], F32))
        kmax2b = en(nc.sbuf_tensor("kmax2b", [P, 32], F32))
        normk = en(nc.sbuf_tensor("normk", [P, 128], F32))
        normsq = en(nc.sbuf_tensor("normsq", [P, 128], F32))
        bTf = en(nc.sbuf_tensor("bTw", [32, 64], F32))
        km8 = en(nc.sbuf_tensor("km8", [8, 1], F32))
        kmrow2 = en(nc.sbuf_tensor("kmrow2", [1, 8], F32))
        kmrow = en(nc.sbuf_tensor("kmrow", [8, 128], F32))
        lnst = en(nc.sbuf_tensor("lnst", [P, 16], F32))
        kacc = en(nc.sbuf_tensor("kacc", [P, 8], F32))
        ps = en(nc.psum_tensor("ps", [P, 4096], F32))
        block = en(nc.Block())
        sems = {nm: Sem(stack.enter_context(nc.semaphore(nm))) for nm in SEM_NAMES}
        g = type("G", (), sems)
        em = Emitter(nc)

        def bfv(t, o, n):
            return t[:, o:o + n].bitcast(BF16)

        xnT = bfv(xnT_, 0, 8192)
        identb = bfv(ident_, 0, 64)
        maskb = bfv(mask_, 0, 64)
        onesb = bfv(onesb_, 0, 64)
        ropeb = bfv(ropebf_, 0, 512)
        sqb = bfv(sqbf_, 0, 256)
        kropeb = bfv(kropebf_, 0, 256)
        bob2 = bob_[0:1, :].bitcast(BF16)
        rbb = rbbf[0:1, :].bitcast(BF16)
        bTb = bTf[0:32, :].bitcast(BF16)

        O_KAUG, O_VALL, O_QWKV, O_XYT, O_WSLAB, O_E4, O_RB = (
            0, 8192, 12352, 20544, 24640, 25664, 27712)
        kaugT = bfv(pool, O_KAUG, 8192)
        vall = bfv(pool, O_VALL, 4160)
        wkb = bfv(pool, O_QWKV, 4096)
        wvb = bfv(pool, O_QWKV + 4096, 4096)
        qaugT = bfv(pool, O_QWKV, 8192)
        xin = [pool[:, O_XYT:O_XYT + 2048], pool[:, O_XYT + 2048:O_XYT + 4096]]
        yt = bfv(pool, O_XYT, 4096)
        wslab = [bfv(pool, O_WSLAB, 512), bfv(pool, O_WSLAB + 512, 512)]
        e4 = [bfv(pool, O_E4, 1024), bfv(pool, O_E4 + 1024, 1024)]
        rb = pool[0:P, O_RB:O_RB + 2048]
        gated = bfv(pool, 0, 16384)
        wgs = [bfv(pool, 16384, 2048), bfv(pool, 18432, 2048)]
        wus = [bfv(pool, 20480, 2048), bfv(pool, 22528, 2048)]
        wds = [bfv(pool, 24576, 2048), bfv(pool, 26624, 2048)]

        def psb(o, n):
            return ps[:, o:o + n]

        def psbf(o, n):
            return ps[:, o:o + n].bitcast(BF16)

        qa = sg[:, 0:512]
        qb = sg[:, 512:1024]
        qc = sg[:, 1024:1536]
        qd = sg[:, 1536:2048]
        xr = [sg[:, 2048:2560], sg[:, 2560:3072]]

        em.op("sync", "dma_start", out=fl[:, :], in_=dflag.unsqueeze(0), sem=g.sFlag, k=16)
        em.op("gpsimd", "dma_start", out=identb, in_=dident, sem=g.sCast, k=16)
        em.op("sync", "dma_start", out=identf[:, :], in_=dident, sem=g.sFlag, k=16)
        em.op("gpsimd", "dma_start", out=maskb, in_=dmask, sem=g.sCast, k=16)
        em.op("gpsimd", "dma_start", out=bob2, in_=dbo.unsqueeze(0), sem=g.sCast, k=16)
        em.op("gpsimd", "dma_start", out=wkb.rearrange("p (i c) -> p i c", i=16),
              in_=dwk.rearrange("(i p) c -> p i c", p=P), sem=g.sCast, k=16)
        em.op("gpsimd", "dma_start", out=wvb.rearrange("p (i c) -> p i c", i=16),
              in_=dwv.rearrange("(i p) c -> p i c", p=P), sem=g.sCast, k=16)
        CAST_WKV = g.sCast.n
        em.op("sync", "dma_start", out=biasb[:, 0:512],
              in_=dbk.unsqueeze(0).partition_broadcast(P), sem=g.sFlag, k=16)
        em.op("sync", "dma_start", out=biasb[:, 512:1024],
              in_=dbv.unsqueeze(0).partition_broadcast(P), sem=g.sFlag, k=16)
        em.op("vector", "memset", onesb, 1.0)
        em.op("vector", "memset", onesf[:, :], 1.0)
        em.op("vector", "memset", normk[:, :], 0.0)
        em.op("vector", "memset", kacc[:, :], 0.0)
        em.op("vector", "memset", kaugT[64:65, :], -1.0)
        vones = vall.rearrange("p (t h c) -> p t h c", t=NT, h=HKV)[:, :, :, 64:65]
        em.op("vector", "memset", vones, 1.0, sem=g.sMean)
        pro_counts = {nm: s.n for nm, s in sems.items()}
        PRO_FLAG = g.sFlag.n
        PRO_MEAN = g.sMean.n

        for par in (0, 1):
            em.cur = par
            for nm, s in sems.items():
                s.n = pro_counts[nm]
            chunks = CHUNKS[par]
            nkv = NKV[par]
            own_blocks = [c * NB + i for c in chunks for i in range(NB)]

            for eng in Emitter.ENGS:
                em.wait(eng, g.sFlag, PRO_FLAG)
                em.wait(eng, g.sMean, PRO_MEAN)
                em.wait(eng, g.sCast, 48)

            wkv_ready = CAST_WKV

            norm_done = {}
            kvevac_done = {}
            tabk_used = {}
            xn_tr_done = {}

            for j in range(nkv):
                pj = j % 2
                if j >= 2:
                    em.wait("sync", g.sMean, norm_done[j - 2])
                em.op("sync", "dma_start", out=xin[pj], in_=dx[j * P:(j + 1) * P, :], sem=g.sX, k=16)
                if j >= 2:
                    em.wait("sync", g.sKVe, tabk_used[j - 2])
                em.op("sync", "dma_start", out=tab_[:, pj * 1024:(pj + 1) * 1024],
                      in_=dtk[j * P:(j + 1) * P, :], sem=g.sX, k=16)
                x_ready = g.sX.n

                ls = pj * 8
                em.wait("scalar", g.sX, x_ready)
                if j >= 2:
                    em.wait("scalar", g.sMean, norm_done[j - 2])
                em.op("scalar", "activation", e4[1], xin[pj], AF.Square, accum_out=lnst[:, ls + 0:ls + 1])
                em.op("scalar", "activation", e4[1], xin[pj], AF.Copy, accum_out=lnst[:, ls + 1:ls + 2], sem=g.sLN)
                em.wait("vector", g.sLN, g.sLN.n)
                em.op("vector", "tensor_scalar_mul", lnst[:, ls + 1:ls + 2], lnst[:, ls + 1:ls + 2], 1.0 / E)
                em.op("vector", "tensor_scalar_mul", lnst[:, ls + 0:ls + 1], lnst[:, ls + 0:ls + 1], 1.0 / E, sem=g.sVs)
                em.wait("vector", g.sVs, g.sVs.n)
                em.op("vector", "tensor_tensor", lnst[:, ls + 2:ls + 3], lnst[:, ls + 1:ls + 2], lnst[:, ls + 1:ls + 2],
                      ALU.mult, sem=g.sVs)
                em.wait("vector", g.sVs, g.sVs.n)
                em.op("vector", "tensor_tensor", lnst[:, ls + 0:ls + 1], lnst[:, ls + 0:ls + 1], lnst[:, ls + 2:ls + 3],
                      ALU.subtract, sem=g.sVs)
                em.wait("vector", g.sVs, g.sVs.n)
                em.op("vector", "tensor_scalar_add", lnst[:, ls + 0:ls + 1], lnst[:, ls + 0:ls + 1], EPS, sem=g.sMean)
                em.wait("scalar", g.sMean, g.sMean.n)
                em.op("scalar", "activation", lnst[:, ls + 3:ls + 4], lnst[:, ls + 0:ls + 1], AF.Sqrt, sem=g.sLN)
                em.wait("vector", g.sLN, g.sLN.n)
                em.op("vector", "reciprocal", lnst[:, ls + 4:ls + 5], lnst[:, ls + 3:ls + 4], sem=g.sVs)
                em.wait("vector", g.sVs, g.sVs.n)
                xnb = e4[0]
                if j >= 1:
                    em.wait("vector", g.sTm, xn_tr_done[j - 1])
                em.op("vector", "tensor_scalar", xnb, xin[pj], lnst[:, ls + 1:ls + 2], lnst[:, ls + 4:ls + 5],
                      ALU.subtract, ALU.mult, sem=g.sMean)
                norm_done[j] = g.sMean.n

                own = j in own_blocks
                owncol = own_blocks.index(j) if own else -1

                em.wait("tensor", g.sMean, norm_done[j])
                if j >= 1:
                    em.wait("tensor", g.sKVe, kvevac_done[j - 1])
                xp_round = {}
                for r in range(4):
                    pb = psbf(1024 + (r % 2) * 256, 256)
                    if (j, r - 2) in xp_round or (j - 1, r + 2) in xp_round:
                        prev = xp_round.get((j, r - 2), xp_round.get((j - 1, r + 2)))
                        em.wait("tensor", g.sXP, prev)
                    for i in range(4):
                        dt = r * 4 + i
                        em.op("tensor", "transpose", pb[:, i * P:(i + 1) * P],
                              xnb[:, dt * P:(dt + 1) * P], identb, sem=(g.sTm if i == 3 else None))
                    if r == 3:
                        xn_tr_done[j] = g.sTm.n
                    em.wait("vector", g.sTm, g.sTm.n)
                    if own:
                        dst = xnT.rearrange("p (d c) -> p d c", d=16)[:, r * 4:r * 4 + 4,
                                                                     owncol * P:(owncol + 1) * P]
                    else:
                        dst = wslab[r % 2][:, 0:512].rearrange("p (d c) -> p d c", d=4)
                    em.op("vector", "tensor_copy", dst, pb.rearrange("p (i c) -> p i c", i=4),
                          sem=g.sXP)
                    xp_round[(j, r)] = g.sXP.n
                    em.wait("tensor", g.sXP, xp_round[(j, r)])
                    if r == 0:
                        em.wait("tensor", g.sCast, wkv_ready)
                    for i in range(4):
                        dt = r * 4 + i
                        if own:
                            lx = xnT[:, dt * 1024 + owncol * P: dt * 1024 + (owncol + 1) * P]
                        else:
                            lx = wslab[r % 2][:, i * P:(i + 1) * P]
                        st, sp = (dt == 0), (dt == 15)
                        em.op("tensor", "matmul", psb(0, 512), lx, wkb[:, dt * 512:(dt + 1) * 512],
                              start=st, stop=sp, sem=(g.sTm if sp else None))
                        em.op("tensor", "matmul", psb(512, 512), lx, wvb[:, dt * 512:(dt + 1) * 512],
                              start=st, stop=sp, sem=(g.sTm if sp else None))
                kv_done = g.sTm.n

                em.wait("vector", g.sTm, kv_done)
                vdst = vall.rearrange("p (t h c) -> p t h c", t=NT, h=HKV)[:, j, :, 0:64]
                em.op("vector", "tensor_tensor", vdst,
                      psb(512, 512).rearrange("p (h c) -> p h c", h=HKV),
                      biasb[:, 512:1024].rearrange("p (h c) -> p h c", h=HKV),
                      ALU.add, sem=g.sKVe)
                em.wait("vector", g.sX, x_ready)
                em.op("vector", "tensor_tensor", qa, psb(0, 512), biasb[:, 0:512], ALU.add,
                      sem=g.sKVe)
                em.wait("scalar", g.sKVe, g.sKVe.n)
                v3 = qa.rearrange("p (h a c) -> p h a c", h=HKV, a=2)
                r3 = qb.rearrange("p (h a c) -> p h a c", h=HKV, a=2)
                em.op("scalar", "copy", r3[:, :, 0, :], v3[:, :, 1, :])
                em.op("scalar", "copy", r3[:, :, 1, :], v3[:, :, 0, :], sem=g.sLN)
                tabc = tab_[:, pj * 1024:pj * 1024 + 512]
                tabs = tab_[:, pj * 1024 + 512:pj * 1024 + 1024]
                em.op("vector", "tensor_tensor", qc, qa, tabc, ALU.mult)
                em.wait("vector", g.sLN, g.sLN.n)
                em.op("vector", "tensor_tensor", qd, qb, tabs, ALU.mult)
                em.op("vector", "tensor_tensor", kropeb, qc, qd, ALU.add, sem=g.sKVe)
                tabk_used[j] = g.sKVe.n
                em.wait("tensor", g.sKVe, tabk_used[j])
                pkb = psbf(2048, 256)
                for i in range(4):
                    em.op("tensor", "transpose", pkb[:, i * P:(i + 1) * P],
                          kropeb[:, i * P:(i + 1) * P], identb, sem=(g.sTm if i == 3 else None))
                em.wait("vector", g.sTm, g.sTm.n)
                kde = kaugT.rearrange("p (h c) -> p h c", h=HKV)
                em.op("vector", "tensor_copy", kde[0:64, 0::2, j * P:(j + 1) * P],
                      pkb[0:64, :].rearrange("p (i c) -> p i c", i=4))
                em.op("vector", "tensor_copy", kde[0:64, 1::2, j * P:(j + 1) * P],
                      pkb[64:128, :].rearrange("p (i c) -> p i c", i=4), sem=g.sKVe)
                em.op("vector", "tensor_tensor", sqb, kropeb, kropeb, ALU.mult)
                em.op("vector", "reduce_sum",
                      normk[:, j * 8:(j + 1) * 8].unsqueeze(2),
                      sqb.rearrange("p (h c) -> p h c", h=HKV), axis=AX.X, sem=g.sVs)
                em.wait("vector", g.sVs, g.sVs.n)
                em.op("vector", "tensor_tensor", kacc[:, :], kacc[:, :],
                      normk[:, j * 8:(j + 1) * 8], ALU.max, sem=g.sKVe)
                kvevac_done[j] = g.sKVe.n

            kv_all_done = g.sKVe.n
            kv_mm_done = g.sTm.n

            em.wait("vector", g.sKVe, g.sKVe.n)
            em.op("vector", "tensor_copy", lnst[:, 0:8], kacc[:, :], sem=g.sMean)
            em.wait("tensor", g.sMean, g.sMean.n)
            em.wait("tensor", g.sKVe, kv_all_done)
            em.op("tensor", "transpose", psb(2048, 128)[0:8, :], lnst[:, 0:8], identf[:, :], sem=g.sTm)
            em.wait("scalar", g.sTm, g.sTm.n)
            em.op("scalar", "copy", kmrow[0:8, :], psb(2048, 128)[0:8, 0:P], sem=g.sLN)
            em.wait("vector", g.sLN, g.sLN.n)
            em.op("vector", "reduce_max", km8[:, 0:1], kmrow[0:8, :], axis=AX.X, sem=g.sVs)
            em.wait("vector", g.sVs, g.sVs.n)
            em.op("vector", "tensor_scalar_mul", km8[:, :], km8[:, :], 1.0609, sem=g.sMean)
            em.wait("gpsimd", g.sMean, g.sMean.n)
            em.op("gpsimd", "dma_start", out=kmrow2[:, :], in_=km8[:, :], sem=g.sB, k=16)
            em.wait("vector", g.sB, g.sB.n)
            for jj in range(4):
                em.op("vector", "tensor_copy",
                      km32.rearrange("p (k j2) -> p k j2", j2=4)[:, :, jj], kmrow2[:, :],
                      sem=(g.sMean if jj == 3 else None))
            em.wait("tensor", g.sMean, g.sMean.n)
            em.op("tensor", "matmul", psb(2048, 32), onesf[0:1, :], km32[0:1, :],
                  start=True, stop=True, skip_group_check=True, sem=g.sTm)
            em.wait("scalar", g.sTm, g.sTm.n)
            em.op("scalar", "copy", kmax2b[:, :], psb(2048, 32), sem=g.sLN)
            kmax_ready = g.sLN.n

            em.wait("sync", g.sKVe, kv_all_done)
            em.op("sync", "dma_start", out=biasb[:, :],
                  in_=dbq.unsqueeze(0).partition_broadcast(P), sem=g.sX, k=16)
            bq_ready = g.sX.n

            prev_chunk_S_done = None
            prev_chunk_out_done = None
            att_pi = 0
            pair_e = {}
            yt_hist = {}
            av_full = {}

            for ci, ch in enumerate(chunks):
                b0 = ch * NB
                njt = b0 + NB

                qgrp_done = {}
                rope_done = {}
                if ci == 1:
                    em.wait("vector", g.sS, prev_chunk_S_done)
                    em.wait("gpsimd", g.sS, prev_chunk_S_done)
                    em.wait("vector", g.sOutW, prev_chunk_out_done)
                tr_done = {}
                evac_hist = []
                for ecg in range(2):
                    for dt in range(16):
                        sl = dt % 2
                        if (ecg * 16 + dt) >= 2:
                            key = (ecg, dt - 2) if dt >= 2 else (ecg - 1, 14 + dt)
                            em.wait("gpsimd", g.sTm, qgrp_done[key])
                        else:
                            em.wait("gpsimd", g.sTm, g.sTm.n)
                        em.op("gpsimd", "dma_start", out=wslab[sl],
                              in_=dwq[dt * P:(dt + 1) * P, ecg * 1024:(ecg + 1) * 1024],
                              sem=g.sW, k=16)
                        em.wait("tensor", g.sW, g.sW.n)
                        if dt == 0:
                            em.wait("tensor", g.sX, bq_ready)
                            if ecg == 1:
                                em.wait("tensor", g.sQT, g.sQT.n)
                            if ci == 0 and ecg == 0:
                                em.wait("tensor", g.sLN, kmax_ready)
                            if ci == 1 and ecg == 0:
                                em.wait("tensor", g.sRB, g.sRB.n)
                                em.wait("tensor", g.sYT, g.sYT.n)
                                em.wait("tensor", g.sOv, g.sOv.n)
                        for e2 in range(2):
                            for bi in range(4):
                                bank = e2 * 4 + bi
                                em.op("tensor", "matmul", psb(bank * 512, 512),
                                      xnT[:, dt * 1024 + (ci * 4 + bi) * P:
                                          dt * 1024 + (ci * 4 + bi) * P + P],
                                      wslab[sl][:, e2 * 512:(e2 + 1) * 512],
                                      start=(dt == 0), stop=(dt == 15),
                                      sem=(g.sTm if (e2 == 1 and bi == 3) else None))
                        qgrp_done[(ecg, dt)] = g.sTm.n
                    qmm = g.sTm.n
                    for bi in range(4):
                        blk = b0 + bi
                        if len(evac_hist) >= 2:
                            em.wait("sync", g.sRope, evac_hist[-2])
                        em.op("sync", "dma_start",
                              out=tab_[:, (bi % 2) * 1024:(bi % 2 + 1) * 1024],
                              in_=dtq[blk * P:(blk + 1) * P, :], sem=g.sX, k=16)
                        tabq_ready = g.sX.n
                        for e2 in range(2):
                            ec = ecg * 2 + e2
                            bank = e2 * 4 + bi
                            em.wait("vector", g.sTm, qmm)
                            em.op("vector", "tensor_tensor", qa, psb(bank * 512, 512),
                                  biasb[:, ec * 512:(ec + 1) * 512], ALU.add, sem=g.sRope)
                            em.wait("scalar", g.sRope, g.sRope.n)
                            v3 = qa.rearrange("p (h a c) -> p h a c", h=8, a=2)
                            r3 = qb.rearrange("p (h a c) -> p h a c", h=8, a=2)
                            em.op("scalar", "copy", r3[:, :, 0, :], v3[:, :, 1, :])
                            em.op("scalar", "copy", r3[:, :, 1, :], v3[:, :, 0, :], sem=g.sLN)
                            em.wait("vector", g.sX, tabq_ready)
                            tq_c = tab_[:, (bi % 2) * 1024:(bi % 2) * 1024 + 512]
                            tq_s = tab_[:, (bi % 2) * 1024 + 512:(bi % 2) * 1024 + 1024]
                            em.op("vector", "tensor_tensor", qc, qa, tq_c, ALU.mult)
                            em.wait("vector", g.sLN, g.sLN.n)
                            rslot = ropeb[:, (bank % 2) * 512:(bank % 2) * 512 + 512]
                            if len(evac_hist) >= 2:
                                em.wait("vector", g.sTm, tr_done[evac_hist[-2]])
                            em.op("vector", "tensor_tensor", qd, qb, tq_s, ALU.mult)
                            em.op("vector", "tensor_tensor", rslot, qc, qd, ALU.add, sem=g.sRope)
                            rope_ms = g.sRope.n
                            em.op("vector", "tensor_tensor", sqb, rslot, rslot, ALU.mult)
                            em.op("vector", "reduce_sum",
                                  normsq.rearrange("p (b c) -> p b c", b=4)[:, bi,
                                                                            ec * 8:(ec + 1) * 8].unsqueeze(2),
                                  sqb.rearrange("p (h c) -> p h c", h=8), axis=AX.X, sem=g.sRope)
                            em.wait("tensor", g.sRope, rope_ms)
                            ptq = psbf(bank * 512, 256)
                            for i in range(4):
                                em.op("tensor", "transpose", ptq[:, i * P:(i + 1) * P],
                                      rslot[:, i * P:(i + 1) * P], identb,
                                      sem=(g.sTm if i == 3 else None))
                            tr_done[rope_ms] = g.sTm.n
                            evac_hist.append(rope_ms)
                            em.wait("vector", g.sTm, g.sTm.n)
                            qde = qaugT.rearrange("p (gh c) -> p gh c", gh=HQ)
                            em.op("vector", "tensor_copy",
                                  qde[0:64, 8 * ec + 0:8 * ec + 8:2, bi * P:(bi + 1) * P],
                                  ptq[0:64, :].rearrange("p (i c) -> p i c", i=4))
                            em.op("vector", "tensor_copy",
                                  qde[0:64, 8 * ec + 1:8 * ec + 8:2, bi * P:(bi + 1) * P],
                                  ptq[64:128, :].rearrange("p (i c) -> p i c", i=4), sem=g.sQT)

                em.wait("vector", g.sLN, kmax_ready)
                em.wait("tensor", g.sQT, g.sQT.n)
                for bi in range(4):
                    nsq = normsq.rearrange("p (b c) -> p b c", b=4)[:, bi, 0:32]
                    em.op("vector", "tensor_tensor", qc[:, 0:32], nsq, kmax2b[:, :], ALU.mult,
                          sem=g.sMean)
                    em.wait("scalar", g.sMean, g.sMean.n)
                    em.op("scalar", "activation", qd[:, 0:32], qc[:, 0:32],
                          AF.Sqrt, sem=g.sLN)
                    em.wait("tensor", g.sLN, g.sLN.n)
                    em.wait("tensor", g.sMean, g.sMean.n)
                    em.op("tensor", "transpose", psb(2048, 128)[0:32, :], qd[:, 0:32], identf[:, :],
                          sem=g.sTm)
                    em.wait("vector", g.sTm, g.sTm.n)
                    em.wait("vector", g.sB, g.sB.n)
                    em.op("vector", "tensor_copy", bTb[0:32, :], psb(2048, 128)[0:32, 0:P],
                          sem=g.sMean)
                    em.wait("gpsimd", g.sMean, g.sMean.n)
                    em.op("gpsimd", "dma_start",
                          out=qaugT[64:65, :].rearrange("p (gh c) -> p gh c", gh=HQ)[:, :,
                                                                                     bi * P:(bi + 1) * P],
                          in_=bTb[0:32, :], sem=g.sB, k=16)

                if debug:
                    em.wait("gpsimd", g.sQT, g.sQT.n)
                    em.op("gpsimd", "dma_start", out=ddbg["qaug2"][ci], in_=qaugT, sem=g.sCast, k=16)
                em.wait("tensor", g.sQT, g.sQT.n)
                em.wait("tensor", g.sB, g.sB.n)
                em.wait("tensor", g.sKVe, kv_all_done)

                def epi_sc(pi):
                    p = pi % 2
                    avb = 2048 + p * 1024
                    em.wait("scalar", g.sAV, g.sAV.n)
                    if pi - 2 in yt_hist:
                        em.wait("scalar", g.sYT, yt_hist[pi - 2])
                    em.op("scalar", "activation", rbf2[0:1, p * 1024:p * 1024 + 1024],
                          ps[64:65, avb:avb + 1024], AF.Ln, sem=g.sRC)
                    em.wait("scalar", g.sRC, g.sRC.n)
                    em.op("scalar", "activation", rb[0:1, p * 1024:p * 1024 + 1024],
                          rbf2[0:1, p * 1024:p * 1024 + 1024], AF.Exp, scale=-1.0, sem=g.sRC)
                    em.wait("scalar", g.sRC, g.sRC.n)
                    em.op("scalar", "copy", rbb[0:1, p * 1024:p * 1024 + 1024],
                          rb[0:1, p * 1024:p * 1024 + 1024], sem=g.sRC)

                def epi_bcast(pi):
                    p = pi % 2
                    avb = 2048 + p * 1024
                    em.wait("tensor", g.sRC, g.sRC.n)
                    for c in range(2):
                        em.op("tensor", "matmul",
                              ps[64:128, avb + c * 512: avb + (c + 1) * 512],
                              onesb[0:1, 0:64],
                              rbb[0:1, p * 1024 + c * 512: p * 1024 + (c + 1) * 512],
                              start=True, stop=True, skip_group_check=True,
                              tile_position=(0, 64),
                              sem=(g.sBC if c == 1 else None))

                def epi_norm(pi, gg, h):
                    p = pi % 2
                    avb = 2048 + p * 1024
                    em.wait("scalar", g.sBC, g.sBC.n)
                    if pi - 2 in yt_hist:
                        em.wait("scalar", g.sYT, yt_hist[pi - 2])
                    em.op("scalar", "copy", rb[0:64, p * 1024:p * 1024 + 1024],
                          ps[64:128, avb:avb + 1024], sem=g.sRB)
                    em.wait("vector", g.sRB, g.sRB.n)
                    for c in range(2):
                        gq = gg * 4 + 2 * h + c
                        em.op("vector", "tensor_tensor",
                              yt[(gq % 2) * 64:(gq % 2) * 64 + 64,
                                 (gq // 2) * 512:(gq // 2) * 512 + 512],
                              ps[0:64, avb + c * 512: avb + (c + 1) * 512],
                              rb[0:64, p * 1024 + c * 512: p * 1024 + (c + 1) * 512],
                              ALU.mult, sem=(g.sYT if c == 1 else None))
                    yt_hist[pi] = g.sYT.n

                pend = None
                for gg in range(8):
                    kh = gg
                    for h in range(2):
                        pi = att_pi
                        p = pi % 2
                        avb = 2048 + p * 1024
                        av_pass = {}
                        for jt in range(njt):
                            i = jt - b0
                            q0 = max(0, i) * P
                            pe = jt % 2
                            sbase = pe * 1024
                            if pe in pair_e:
                                em.wait("tensor", g.sE, pair_e[pe])
                            for c in range(2):
                                gq = gg * 4 + 2 * h + c
                                em.op("tensor", "matmul",
                                      ps[0:128, sbase + c * 512 + q0: sbase + (c + 1) * 512],
                                      kaugT[0:65, kh * 2048 + jt * P: kh * 2048 + (jt + 1) * P],
                                      qaugT[0:65, gq * 512 + q0: (gq + 1) * 512],
                                      start=True, stop=True, skip_group_check=True,
                                      sem=(g.sS if c == 1 else None))
                            s_cnt = g.sS.n
                            if jt == 0 and pend is not None:
                                epi_bcast(pend[0])
                            em.wait("scalar", g.sS, s_cnt)
                            if jt >= 2:
                                em.wait("scalar", g.sAV, av_pass[jt - 2])
                            elif pi - 2 in av_full:
                                em.wait("scalar", g.sAV, av_full[pi - 2])
                            eview = e4[pe].rearrange("p (s c) -> p s c", s=4)[:, 2 * h:2 * h + 2, q0:512]
                            sview = ps[:, sbase:sbase + 1024].rearrange("p (s c) -> p s c", s=2)[:, :, q0:512]
                            em.op("scalar", "activation", eview, sview, AF.Exp, sem=g.sE)
                            e_cnt = g.sE.n
                            pair_e[pe] = e_cnt
                            m_cnt = 0
                            if i >= 0:
                                em.wait("vector", g.sE, e_cnt)
                                for c in range(2):
                                    s4 = 2 * h + c
                                    em.op("vector", "tensor_tensor",
                                          e4[pe][:, s4 * 512 + q0: s4 * 512 + q0 + P],
                                          e4[pe][:, s4 * 512 + q0: s4 * 512 + q0 + P],
                                          maskb, ALU.mult,
                                          sem=(g.sM if c == 1 else None))
                                m_cnt = g.sM.n
                            if jt == 0 and pend is not None:
                                epi_norm(*pend)
                                pend = None
                            if i >= 0:
                                em.wait("tensor", g.sM, m_cnt)
                            else:
                                em.wait("tensor", g.sE, e_cnt)
                            if jt == 0 and pi - 2 in yt_hist:
                                em.wait("tensor", g.sYT, yt_hist[pi - 2])
                            for c in range(2):
                                s4 = 2 * h + c
                                em.op("tensor", "matmul",
                                      ps[0:65, avb + c * 512 + q0: avb + (c + 1) * 512],
                                      vall[0:128, jt * 520 + kh * 65: jt * 520 + kh * 65 + 65],
                                      e4[pe][:, s4 * 512 + q0: (s4 + 1) * 512],
                                      start=(jt == 0), stop=(jt == njt - 1), skip_group_check=True,
                                      sem=(g.sAV if c == 1 else None))
                            av_pass[jt] = g.sAV.n
                        av_full[pi] = g.sAV.n
                        epi_sc(pi)
                        pend = (pi, gg, h)
                        att_pi += 1
                epi_bcast(pend[0])
                epi_norm(*pend)
                pend = None
                prev_chunk_S_done = g.sS.n

                em.wait("tensor", g.sYT, g.sYT.n)
                wogrp_done = {}
                ov_hist = []
                outw_hist = []
                for ecg in range(2):
                    for et in range(16):
                        sl = et % 2
                        if (ecg * 16 + et) >= 2:
                            key = (ecg, et - 2) if et >= 2 else (ecg - 1, 14 + et)
                            em.wait("gpsimd", g.sTm, wogrp_done[key])
                        else:
                            em.wait("gpsimd", g.sRB, g.sRB.n)
                            em.wait("gpsimd", g.sQT, g.sQT.n)
                        em.op("gpsimd", "dma_start", out=wslab[sl],
                              in_=dwo[et * P:(et + 1) * P, ecg * 1024:(ecg + 1) * 1024],
                              sem=g.sW, k=16)
                        em.wait("tensor", g.sW, g.sW.n)
                        if et == 0:
                            if ecg == 1:
                                em.wait("tensor", g.sOv, g.sOv.n)
                            else:
                                em.wait("tensor", g.sRB, g.sRB.n)
                                em.wait("tensor", g.sYT, g.sYT.n)
                        for e2 in range(2):
                            for bi in range(4):
                                bank = e2 * 4 + bi
                                em.op("tensor", "matmul", psb(bank * 512, 512),
                                      yt[:, et * 512 + bi * P: et * 512 + (bi + 1) * P],
                                      wslab[sl][:, e2 * 512:(e2 + 1) * 512],
                                      start=(et == 0), stop=False, skip_group_check=True,
                                      sem=(g.sTm if (e2 == 1 and bi == 3) else None))
                        wogrp_done[(ecg, et)] = g.sTm.n
                    for e2 in range(2):
                        ec = ecg * 2 + e2
                        for bi in range(4):
                            bank = e2 * 4 + bi
                            blk = b0 + bi
                            em.op("tensor", "matmul", psb(bank * 512, 512),
                                  onesb[0:1, :], bob2[0:1, ec * 512:(ec + 1) * 512],
                                  start=False, stop=True, skip_group_check=True, sem=g.sTm)
                            womm = g.sTm.n
                            inst = len(ov_hist)
                            if inst >= 2:
                                em.wait("sync", g.sOv, ov_hist[-2])
                            em.op("sync", "dma_start", out=xr[inst % 2],
                                  in_=dx[blk * P:(blk + 1) * P, ec * 512:(ec + 1) * 512],
                                  sem=g.sX, k=16)
                            em.wait("vector", g.sX, g.sX.n)
                            em.wait("vector", g.sTm, womm)
                            if len(outw_hist) >= 2:
                                em.wait("vector", g.sOutW, outw_hist[-2])
                            em.op("vector", "tensor_tensor", qa if inst % 2 == 0 else qb,
                                  psb(bank * 512, 512), xr[inst % 2], ALU.add, sem=g.sOv)
                            ov_hist.append(g.sOv.n)
                            em.wait("sync", g.sOv, g.sOv.n)
                            em.op("sync", "dma_start",
                                  out=dout[(ci * 4 + bi) * P:(ci * 4 + bi + 1) * P,
                                           ec * 512:(ec + 1) * 512],
                                  in_=qa if inst % 2 == 0 else qb, sem=g.sOutW, k=16)
                            outw_hist.append(g.sOutW.n)
                prev_chunk_out_done = g.sOutW.n

            all_attn_S = g.sS.n
            all_out_w = g.sOutW.n

            if debug:
                em.wait("gpsimd", g.sOutW, all_out_w)
                for _src, _dst in ((xnT, ddbg["xnT"]), (kaugT, ddbg["kaug"]), (vall, ddbg["vall"]),
                                   (qaugT, ddbg["qaug"]), (yt, ddbg["yt"])):
                    em.op("gpsimd", "dma_start", out=_dst, in_=_src, sem=g.sCast, k=16)
                em.op("gpsimd", "dma_start", out=ddbg["misc"][:, 0:32], in_=kmax2b[:, :], sem=g.sCast, k=16)
                em.op("gpsimd", "dma_start", out=ddbg["misc"][:, 32:160], in_=normsq[:, :], sem=g.sCast, k=16)
                em.wait("gpsimd", g.sCast, g.sCast.n)

            em.wait("gpsimd", g.sOutW, all_out_w)
            em.wait("gpsimd", g.sS, all_attn_S)
            em.wait("vector", g.sAV, g.sAV.n)
            em.wait("vector", g.sS, all_attn_S)
            em.wait("vector", g.sOutW, all_out_w)
            em.wait("scalar", g.sOutW, all_out_w)
            em.wait("scalar", g.sS, all_attn_S)
            em.wait("scalar", g.sYT, g.sYT.n)
            ffn_grp_done = {}
            ga_hist = []
            de_hist = []
            accum_hist = []
            for hi in range(2):
                for hg in range(32):
                    sl = hg % 2
                    if (hi, hg - 2) in ffn_grp_done or (hi - 1, 30 + hg) in ffn_grp_done:
                        prevk = (hi, hg - 2) if hg >= 2 else (hi - 1, 30 + hg)
                        em.wait("gpsimd", g.sTm, ffn_grp_done[prevk])
                    em.op("gpsimd", "dma_start", out=wgs[sl], in_=dwg[hg],
                          sem=g.sW, k=16)
                    em.op("gpsimd", "dma_start", out=wus[sl], in_=dwu[hg],
                          sem=g.sW, k=16)
                    em.wait("tensor", g.sW, g.sW.n)
                    for h2 in range(2):
                        ht = hg * 2 + h2
                        gbk = (ht % 2) * 2
                        ubk = gbk + 1
                        if len(ga_hist) >= 2:
                            em.wait("tensor", g.sGa, ga_hist[-2])
                        elif hi == 0 and hg == 0 and h2 == 0:
                            em.wait("tensor", g.sOv, g.sOv.n)
                        for dt in range(16):
                            em.op("tensor", "matmul", psb(gbk * 512, 512),
                                  wgs[sl][:, dt * 256 + h2 * P: dt * 256 + (h2 + 1) * P],
                                  xnT[:, dt * 1024 + hi * 512: dt * 1024 + (hi + 1) * 512],
                                  start=(dt == 0), stop=(dt == 15),
                                  sem=(g.sTm if dt == 15 else None))
                        for dt in range(16):
                            em.op("tensor", "matmul", psb(ubk * 512, 512),
                                  wus[sl][:, dt * 256 + h2 * P: dt * 256 + (h2 + 1) * P],
                                  xnT[:, dt * 1024 + hi * 512: dt * 1024 + (hi + 1) * 512],
                                  start=(dt == 0), stop=(dt == 15),
                                  sem=(g.sTm if dt == 15 else None))
                        em.wait("scalar", g.sTm, g.sTm.n)
                        em.op("scalar", "activation", qd, psb(gbk * 512, 512), AF.Silu, sem=g.sSi)
                        em.wait("vector", g.sSi, g.sSi.n)
                        em.op("vector", "tensor_tensor", gated[:, ht * 512:(ht + 1) * 512],
                              qd, psb(ubk * 512, 512), ALU.mult, sem=g.sGa)
                        ga_hist.append(g.sGa.n)
                    ffn_grp_done[(hi, hg)] = g.sTm.n
                em.wait("tensor", g.sGa, g.sGa.n)
                wd_done = {}
                for ec in range(4):
                    for hgrp in range(8):
                        sl = hgrp % 2
                        if (ec, hgrp - 2) in wd_done or (ec - 1, 6 + hgrp) in wd_done:
                            prevk = (ec, hgrp - 2) if hgrp >= 2 else (ec - 1, 6 + hgrp)
                            em.wait("gpsimd", g.sTm, wd_done[prevk])
                        em.op("gpsimd", "dma_start", out=wds[sl], in_=dwd[ec, hgrp],
                              sem=g.sW, k=16)
                        em.wait("tensor", g.sW, g.sW.n)
                        if hgrp == 0 and len(de_hist) >= 4:
                            em.wait("tensor", g.sDe, de_hist[-4])
                        for hh in range(8):
                            ht = hgrp * 8 + hh
                            for bi in range(4):
                                em.op("tensor", "matmul", psb((4 + bi) * 512, 512),
                                      gated[:, ht * 512 + bi * P: ht * 512 + (bi + 1) * P],
                                      wds[sl][:, hh * 512:(hh + 1) * 512],
                                      start=(ht == 0), stop=(ht == 63), skip_group_check=True,
                                      sem=(g.sTm if (hh == 7 and bi == 3) else None))
                        wd_done[(ec, hgrp)] = g.sTm.n
                    em.wait("scalar", g.sTm, g.sTm.n)
                    for bi in range(4):
                        inst = len(de_hist)
                        if inst >= 2:
                            em.wait("scalar", g.sOutF, accum_hist[-2])
                        em.op("scalar", "copy", qa if inst % 2 == 0 else qb,
                              psb((4 + bi) * 512, 512), sem=g.sDe)
                        de_hist.append(g.sDe.n)
                        em.wait("gpsimd", g.sDe, g.sDe.n)
                        if ffn_split:
                            em.op("gpsimd", "dma_start",
                                  out=dffn[(hi * 4 + bi) * P:(hi * 4 + bi + 1) * P,
                                           ec * 512:(ec + 1) * 512],
                                  in_=qa if inst % 2 == 0 else qb,
                                  sem=g.sOutF, k=16)
                        else:
                            em.op("gpsimd", "dma_start",
                                  out=dout[(hi * 4 + bi) * P:(hi * 4 + bi + 1) * P,
                                           ec * 512:(ec + 1) * 512],
                                  in_=qa if inst % 2 == 0 else qb, accum_op=ALU.add,
                                  sem=g.sOutF, k=16)
                        accum_hist.append(g.sOutF.n)

            em.wait("gpsimd", g.sOutF, g.sOutF.n)
            em.wait("gpsimd", g.sOutW, g.sOutW.n)
            em.wait("sync", g.sOutF, g.sOutF.n)
            em.wait("sync", g.sOutW, g.sOutW.n)

        def emit_engine(name):
            eng = getattr(nc, name)
            for fn in em.pre[name]:
                fn()
            eng.wait_ge(sems["sFlag"].h, 16)
            with eng.register("r_" + name) as r:
                eng.reg_load(r, fl[0:1, 0:1])
                with eng.If_eq(r, 0):
                    for fn in em.bod[0][name]:
                        fn()
                with eng.Else():
                    for fn in em.bod[1][name]:
                        fn()

        @block.sync
        def _(s):
            emit_engine("sync")

        @block.tensor
        def _(t):
            emit_engine("tensor")

        @block.vector
        def _(v):
            emit_engine("vector")

        @block.scalar
        def _(s):
            emit_engine("scalar")

        @block.gpsimd
        def _(gp):
            emit_engine("gpsimd")

    return nc


def _host_inputs(inputs):
    x = np.asarray(inputs["L0_attn_x"], np.float32)
    wq = np.ascontiguousarray(np.asarray(inputs["L0_attn_Wq"], np.float32))
    wk = np.ascontiguousarray(np.asarray(inputs["L0_attn_Wk"], np.float32))
    wv = np.ascontiguousarray(np.asarray(inputs["L0_attn_Wv"], np.float32))
    wo = np.ascontiguousarray(np.asarray(inputs["L0_attn_Wo"], np.float32))
    bq = np.ascontiguousarray(np.asarray(inputs["L0_attn_bq"], np.float32))
    bk = np.ascontiguousarray(np.asarray(inputs["L0_attn_bk"], np.float32))
    bv = np.ascontiguousarray(np.asarray(inputs["L0_attn_bv"], np.float32))
    bo = np.ascontiguousarray(np.asarray(inputs["L0_attn_bo"], np.float32))
    wg = np.asarray(inputs["L0_ffn_w_gate"], np.float32)
    wu = np.asarray(inputs["L0_ffn_w_up"], np.float32)
    wd = np.asarray(inputs["L0_ffn_w_down"], np.float32)

    wg2 = np.ascontiguousarray(wg.reshape(32, 256, 16, 128).transpose(0, 3, 2, 1).reshape(32, 128, 4096))
    wu2 = np.ascontiguousarray(wu.reshape(32, 256, 16, 128).transpose(0, 3, 2, 1).reshape(32, 128, 4096))
    wd4 = np.ascontiguousarray(
        wd.reshape(4, 512, 8, 8, 128).transpose(0, 2, 4, 3, 1).reshape(4, 8, 128, 4096))

    pos = np.arange(1, T + 1, dtype=np.float64)[:, None]
    freqs = 10000.0 ** (-(2.0 * np.arange(D // 2, dtype=np.float64)) / D)
    th = pos * freqs
    cos = np.cos(th)
    sin = np.sin(th)
    Ct = np.concatenate([cos, cos], 1)
    St = np.concatenate([-sin, sin], 1)
    Crep = np.tile(Ct, (1, 8))
    Srep = np.tile(St, (1, 8))
    tk = np.ascontiguousarray(np.concatenate([Crep, Srep], 1).astype(np.float32))
    tq = np.ascontiguousarray((np.concatenate([Crep, Srep], 1) * 0.125).astype(np.float32))
    identm = np.eye(P, dtype=np.float32)
    maskm = np.ascontiguousarray(np.triu(np.ones((P, P), np.float32)))

    in_maps = []
    for c in range(8):
        b, par = c // 2, c % 2
        in_maps.append({
            "x": np.ascontiguousarray(x[b]),
            "wq": wq, "wk": wk, "wv": wv, "wo": wo,
            "bq": bq, "bk": bk, "bv": bv, "bo": bo,
            "wg2": wg2, "wu2": wu2, "wd4": wd4,
            "tq": tq, "tk": tk, "identm": identm, "maskm": maskm,
            "flag": np.array([par], np.int32),
        })
    return in_maps


_NC_CACHE = {}


def kernel(**inputs):
    in_maps = _host_inputs(inputs)
    if "nc" not in _NC_CACHE:
        _NC_CACHE["nc"] = build_nc(debug=False)
    nc = _NC_CACHE["nc"]
    res = run_bass_kernel_spmd(nc, in_maps, core_ids=list(range(8)))
    out = np.zeros((4, T, E), np.float32)
    for c in range(8):
        b, par = c // 2, c % 2
        o = np.asarray(res.results[c]["out"], np.float32)
        lo, hi = CHUNKS[par]
        out[b, lo * 512:(lo + 1) * 512] = o[0:512]
        out[b, hi * 512:(hi + 1) * 512] = o[512:1024]
    return out


# version 26
# speedup vs baseline: 1.1193x; 1.1193x over previous
import contextlib
import numpy as np
import ml_dtypes
import concourse.bass as bass
import concourse.mybir as mybir
from concourse.bass_utils import run_bass_kernel_spmd

F32 = mybir.dt.float32
BF16 = mybir.dt.bfloat16
I32 = mybir.dt.int32
AF = mybir.ActivationFunctionType
ALU = mybir.AluOpType
AX = mybir.AxisListType

P = 128
T = 2048
E = 2048
DKV = 512
HQ, HKV, D = 32, 8, 64
NT = 16
NB = 4
EPS = 1e-5

CHUNKS = {0: (0, 3), 1: (1, 2)}
NKV = {0: 16, 1: 12}

SEM_NAMES = [
    "sFlag", "sX", "sW", "sLN", "sMean", "sXP", "sTm", "sKVe", "sRope",
    "sQT", "sB", "sS", "sE", "sM", "sAV", "sRC", "sBC", "sRB", "sYT",
    "sOv", "sOutW", "sOutF", "sSi", "sGa", "sDe", "sVs", "sCast",
]


class Sem:
    def __init__(self, h):
        self.h = h
        self.n = 0


class Emitter:
    ENGS = ("sync", "tensor", "vector", "scalar", "gpsimd")

    def __init__(self, nc):
        self.nc = nc
        self.pre = {e: [] for e in self.ENGS}
        self.bod = [{e: [] for e in self.ENGS}, {e: [] for e in self.ENGS}]
        self.cur = None

    def _q(self, eng):
        return self.pre[eng] if self.cur is None else self.bod[self.cur][eng]

    def op(self, eng, meth, *args, sem=None, k=1, **kw):
        nc = self.nc

        def fn(eng=eng, meth=meth, args=args, kw=kw, sem=sem, k=k):
            ins = getattr(getattr(nc, eng), meth)(*args, **kw)
            if sem is not None:
                ins.then_inc(sem.h, k)
        if sem is not None:
            sem.n += k
        self._q(eng).append(fn)
        return sem.n if sem is not None else None

    def wait(self, eng, sem, val=None):
        nc = self.nc
        v = sem.n if val is None else val
        if v <= 0:
            return

        def fn(eng=eng, sem=sem, v=v):
            getattr(nc, eng).wait_ge(sem.h, v)
        self._q(eng).append(fn)


def build_nc(debug=False, ffn_split=False):
    nc = bass.Bass("TRN2", target_bir_lowering=False)

    dx = nc.dram_tensor("x", [T, E], F32, kind="ExternalInput").ap()
    dwq = nc.dram_tensor("wq", [E, E], F32, kind="ExternalInput").ap()
    dwk = nc.dram_tensor("wk", [E, DKV], F32, kind="ExternalInput").ap()
    dwv = nc.dram_tensor("wv", [E, DKV], F32, kind="ExternalInput").ap()
    dwo = nc.dram_tensor("wo", [E, E], F32, kind="ExternalInput").ap()
    dbq = nc.dram_tensor("bq", [E], F32, kind="ExternalInput").ap()
    dbk = nc.dram_tensor("bk", [DKV], F32, kind="ExternalInput").ap()
    dbv = nc.dram_tensor("bv", [DKV], F32, kind="ExternalInput").ap()
    dbo = nc.dram_tensor("bo", [E], F32, kind="ExternalInput").ap()
    dwg = nc.dram_tensor("wg2", [32, 128, 4096], F32, kind="ExternalInput").ap()
    dwu = nc.dram_tensor("wu2", [32, 128, 4096], F32, kind="ExternalInput").ap()
    dwd = nc.dram_tensor("wd4", [4, 8, 128, 4096], F32, kind="ExternalInput").ap()
    dtq = nc.dram_tensor("tq", [T, 1024], F32, kind="ExternalInput").ap()
    dtk = nc.dram_tensor("tk", [T, 1024], F32, kind="ExternalInput").ap()
    dident = nc.dram_tensor("identm", [P, P], F32, kind="ExternalInput").ap()
    dmask = nc.dram_tensor("maskm", [P, P], F32, kind="ExternalInput").ap()
    dflag = nc.dram_tensor("flag", [1], I32, kind="ExternalInput").ap()
    dout = nc.dram_tensor("out", [1024, E], F32, kind="ExternalOutput").ap()
    ddbg = {}
    if debug:
        for nm, w in (("xnT", 16384), ("kaug", 16384), ("vall", 8320),
                      ("qaug", 16384), ("yt", 8192), ("misc", 512)):
            ddbg[nm] = nc.dram_tensor("dbg_" + nm, [P, w], F32, kind="ExternalOutput").ap()
        ddbg["qaug2"] = nc.dram_tensor("dbg_qaug2", [2, P, 16384], F32, kind="ExternalOutput").ap()
    dffn = None
    if ffn_split:
        dffn = nc.dram_tensor("dbg_ffn", [1024, E], F32, kind="ExternalOutput").ap()

    POOLW = 29824

    with contextlib.ExitStack() as stack:
        en = stack.enter_context
        pool = en(nc.sbuf_tensor("pool", [P, POOLW], F32))
        xnT_ = en(nc.sbuf_tensor("xnTw", [P, 8192], F32))
        tab_ = en(nc.sbuf_tensor("tabw", [P, 2048], F32))
        biasb = en(nc.sbuf_tensor("biasb", [P, 2048], F32))
        sg = en(nc.sbuf_tensor("sg", [P, 3072], F32))
        ropebf_ = en(nc.sbuf_tensor("ropew", [P, 512], F32))
        sqbf_ = en(nc.sbuf_tensor("sqw", [P, 256], F32))
        kropebf_ = en(nc.sbuf_tensor("kropew", [P, 256], F32))
        bob_ = en(nc.sbuf_tensor("bobw", [1, 1024], F32))
        rbbf = en(nc.sbuf_tensor("rbbf", [1, 1024], F32))
        rbf2 = en(nc.sbuf_tensor("rbf2", [1, 2048], F32))
        fl = en(nc.sbuf_tensor("fl", [1, 1], I32))
        ident_ = en(nc.sbuf_tensor("identw", [P, 64], F32))
        identf = en(nc.sbuf_tensor("identf", [P, 128], F32))
        mask_ = en(nc.sbuf_tensor("maskw", [P, 64], F32))
        onesb_ = en(nc.sbuf_tensor("onesbw", [1, 64], F32))
        onesf = en(nc.sbuf_tensor("onesf", [1, 128], F32))
        km32 = en(nc.sbuf_tensor("km32", [1, 32], F32))
        kmax2b = en(nc.sbuf_tensor("kmax2b", [P, 32], F32))
        normk = en(nc.sbuf_tensor("normk", [P, 128], F32))
        normsq = en(nc.sbuf_tensor("normsq", [P, 128], F32))
        bTf = en(nc.sbuf_tensor("bTw", [32, 64], F32))
        km8 = en(nc.sbuf_tensor("km8", [8, 1], F32))
        kmrow2 = en(nc.sbuf_tensor("kmrow2", [1, 8], F32))
        kmrow = en(nc.sbuf_tensor("kmrow", [8, 128], F32))
        lnst = en(nc.sbuf_tensor("lnst", [P, 16], F32))
        kacc = en(nc.sbuf_tensor("kacc", [P, 8], F32))
        ps = en(nc.psum_tensor("ps", [P, 4096], F32))
        block = en(nc.Block())
        sems = {nm: Sem(stack.enter_context(nc.semaphore(nm))) for nm in SEM_NAMES}
        g = type("G", (), sems)
        em = Emitter(nc)

        def bfv(t, o, n):
            return t[:, o:o + n].bitcast(BF16)

        xnT = bfv(xnT_, 0, 8192)
        identb = bfv(ident_, 0, 64)
        maskb = bfv(mask_, 0, 64)
        onesb = bfv(onesb_, 0, 64)
        ropeb = bfv(ropebf_, 0, 512)
        sqb = bfv(sqbf_, 0, 256)
        kropeb = bfv(kropebf_, 0, 256)
        bob2 = bob_[0:1, :].bitcast(BF16)
        rbb = rbbf[0:1, :].bitcast(BF16)
        bTb = bTf[0:32, :].bitcast(BF16)

        O_KAUG, O_VALL, O_QWKV, O_XYT, O_WSLAB, O_E4, O_RB = (
            0, 8192, 12352, 20544, 24640, 25664, 27712)
        kaugT = bfv(pool, O_KAUG, 8192)
        vall = bfv(pool, O_VALL, 4160)
        wkb = bfv(pool, O_QWKV, 4096)
        wvb = bfv(pool, O_QWKV + 4096, 4096)
        qaugT = bfv(pool, O_QWKV, 8192)
        xin = [pool[:, O_XYT:O_XYT + 2048], pool[:, O_XYT + 2048:O_XYT + 4096]]
        yt = bfv(pool, O_XYT, 4096)
        wslab = [bfv(pool, O_WSLAB, 512), bfv(pool, O_WSLAB + 512, 512)]
        e4 = [bfv(pool, O_E4, 1024), bfv(pool, O_E4 + 1024, 1024)]
        rb = pool[0:P, O_RB:O_RB + 2048]
        gated = bfv(pool, 0, 16384)
        wgs = [bfv(pool, 16384, 2048), bfv(pool, 18432, 2048)]
        wus = [bfv(pool, 20480, 2048), bfv(pool, 22528, 2048)]
        wds = [bfv(pool, 24576, 2048), bfv(pool, 26624, 2048)]

        def psb(o, n):
            return ps[:, o:o + n]

        def psbf(o, n):
            return ps[:, o:o + n].bitcast(BF16)

        qa = sg[:, 0:512]
        qb = sg[:, 512:1024]
        qc = sg[:, 1024:1536]
        qd = sg[:, 1536:2048]
        xr = [sg[:, 2048:2560], sg[:, 2560:3072]]
        xr4 = [sg[:, 2048:2560], sg[:, 2560:3072], qc, qd]

        em.op("sync", "dma_start", out=fl[:, :], in_=dflag.unsqueeze(0), sem=g.sFlag, k=16)
        em.op("gpsimd", "dma_start", out=identb, in_=dident, sem=g.sCast, k=16)
        em.op("sync", "dma_start", out=identf[:, :], in_=dident, sem=g.sFlag, k=16)
        em.op("gpsimd", "dma_start", out=maskb, in_=dmask, sem=g.sCast, k=16)
        em.op("gpsimd", "dma_start", out=bob2, in_=dbo.unsqueeze(0), sem=g.sCast, k=16)
        em.op("gpsimd", "dma_start", out=wkb.rearrange("p (i c) -> p i c", i=16),
              in_=dwk.rearrange("(i p) c -> p i c", p=P), sem=g.sCast, k=16)
        em.op("gpsimd", "dma_start", out=wvb.rearrange("p (i c) -> p i c", i=16),
              in_=dwv.rearrange("(i p) c -> p i c", p=P), sem=g.sCast, k=16)
        CAST_WKV = g.sCast.n
        em.op("sync", "dma_start", out=biasb[:, 0:512],
              in_=dbk.unsqueeze(0).partition_broadcast(P), sem=g.sFlag, k=16)
        em.op("sync", "dma_start", out=biasb[:, 512:1024],
              in_=dbv.unsqueeze(0).partition_broadcast(P), sem=g.sFlag, k=16)
        em.op("vector", "memset", onesb, 1.0)
        em.op("vector", "memset", onesf[:, :], 1.0)
        em.op("vector", "memset", normk[:, :], 0.0)
        em.op("vector", "memset", kacc[:, :], 0.0)
        em.op("vector", "memset", kaugT[64:65, :], -1.0)
        vones = vall.rearrange("p (t h c) -> p t h c", t=NT, h=HKV)[:, :, :, 64:65]
        em.op("vector", "memset", vones, 1.0, sem=g.sMean)
        pro_counts = {nm: s.n for nm, s in sems.items()}
        PRO_FLAG = g.sFlag.n
        PRO_MEAN = g.sMean.n

        for par in (0, 1):
            em.cur = par
            for nm, s in sems.items():
                s.n = pro_counts[nm]
            chunks = CHUNKS[par]
            nkv = NKV[par]
            own_blocks = [c * NB + i for c in chunks for i in range(NB)]

            for eng in Emitter.ENGS:
                em.wait(eng, g.sFlag, PRO_FLAG)
                em.wait(eng, g.sMean, PRO_MEAN)
                em.wait(eng, g.sCast, 48)

            wkv_ready = CAST_WKV

            norm_done = {}
            kvevac_done = {}
            tabk_used = {}
            xn_tr_done = {}

            for j in range(nkv):
                pj = j % 2
                if j >= 2:
                    em.wait("sync", g.sMean, norm_done[j - 2])
                em.op("sync", "dma_start", out=xin[pj], in_=dx[j * P:(j + 1) * P, :], sem=g.sX, k=16)
                if j >= 2:
                    em.wait("sync", g.sKVe, tabk_used[j - 2])
                em.op("sync", "dma_start", out=tab_[:, pj * 1024:(pj + 1) * 1024],
                      in_=dtk[j * P:(j + 1) * P, :], sem=g.sX, k=16)
                x_ready = g.sX.n

                ls = pj * 8
                em.wait("scalar", g.sX, x_ready)
                if j >= 2:
                    em.wait("scalar", g.sMean, norm_done[j - 2])
                em.op("scalar", "activation", e4[1], xin[pj], AF.Square, accum_out=lnst[:, ls + 0:ls + 1])
                em.op("scalar", "activation", e4[1], xin[pj], AF.Copy, accum_out=lnst[:, ls + 1:ls + 2], sem=g.sLN)
                em.wait("vector", g.sLN, g.sLN.n)
                em.op("vector", "tensor_scalar_mul", lnst[:, ls + 1:ls + 2], lnst[:, ls + 1:ls + 2], 1.0 / E)
                em.op("vector", "tensor_scalar_mul", lnst[:, ls + 0:ls + 1], lnst[:, ls + 0:ls + 1], 1.0 / E, sem=g.sVs)
                em.wait("vector", g.sVs, g.sVs.n)
                em.op("vector", "tensor_tensor", lnst[:, ls + 2:ls + 3], lnst[:, ls + 1:ls + 2], lnst[:, ls + 1:ls + 2],
                      ALU.mult, sem=g.sVs)
                em.wait("vector", g.sVs, g.sVs.n)
                em.op("vector", "tensor_tensor", lnst[:, ls + 0:ls + 1], lnst[:, ls + 0:ls + 1], lnst[:, ls + 2:ls + 3],
                      ALU.subtract, sem=g.sVs)
                em.wait("vector", g.sVs, g.sVs.n)
                em.op("vector", "tensor_scalar_add", lnst[:, ls + 0:ls + 1], lnst[:, ls + 0:ls + 1], EPS, sem=g.sMean)
                em.wait("scalar", g.sMean, g.sMean.n)
                em.op("scalar", "activation", lnst[:, ls + 3:ls + 4], lnst[:, ls + 0:ls + 1], AF.Sqrt, sem=g.sLN)
                em.wait("vector", g.sLN, g.sLN.n)
                em.op("vector", "reciprocal", lnst[:, ls + 4:ls + 5], lnst[:, ls + 3:ls + 4], sem=g.sVs)
                em.wait("vector", g.sVs, g.sVs.n)
                xnb = e4[0]
                if j >= 1:
                    em.wait("vector", g.sTm, xn_tr_done[j - 1])
                em.op("vector", "tensor_scalar", xnb, xin[pj], lnst[:, ls + 1:ls + 2], lnst[:, ls + 4:ls + 5],
                      ALU.subtract, ALU.mult, sem=g.sMean)
                norm_done[j] = g.sMean.n

                own = j in own_blocks
                owncol = own_blocks.index(j) if own else -1

                em.wait("tensor", g.sMean, norm_done[j])
                if j >= 1:
                    em.wait("tensor", g.sKVe, kvevac_done[j - 1])
                xp_round = {}
                for r in range(4):
                    pb = psbf(1024 + (r % 2) * 256, 256)
                    if (j, r - 2) in xp_round or (j - 1, r + 2) in xp_round:
                        prev = xp_round.get((j, r - 2), xp_round.get((j - 1, r + 2)))
                        em.wait("tensor", g.sXP, prev)
                    for i in range(4):
                        dt = r * 4 + i
                        em.op("tensor", "transpose", pb[:, i * P:(i + 1) * P],
                              xnb[:, dt * P:(dt + 1) * P], identb, sem=(g.sTm if i == 3 else None))
                    if r == 3:
                        xn_tr_done[j] = g.sTm.n
                    em.wait("vector", g.sTm, g.sTm.n)
                    if own:
                        dst = xnT.rearrange("p (d c) -> p d c", d=16)[:, r * 4:r * 4 + 4,
                                                                     owncol * P:(owncol + 1) * P]
                    else:
                        dst = wslab[r % 2][:, 0:512].rearrange("p (d c) -> p d c", d=4)
                    em.op("vector", "tensor_copy", dst, pb.rearrange("p (i c) -> p i c", i=4),
                          sem=g.sXP)
                    xp_round[(j, r)] = g.sXP.n
                    em.wait("tensor", g.sXP, xp_round[(j, r)])
                    if r == 0:
                        em.wait("tensor", g.sCast, wkv_ready)
                    for i in range(4):
                        dt = r * 4 + i
                        if own:
                            lx = xnT[:, dt * 1024 + owncol * P: dt * 1024 + (owncol + 1) * P]
                        else:
                            lx = wslab[r % 2][:, i * P:(i + 1) * P]
                        st, sp = (dt == 0), (dt == 15)
                        em.op("tensor", "matmul", psb(0, 512), lx, wkb[:, dt * 512:(dt + 1) * 512],
                              start=st, stop=sp, sem=(g.sTm if sp else None))
                        em.op("tensor", "matmul", psb(512, 512), lx, wvb[:, dt * 512:(dt + 1) * 512],
                              start=st, stop=sp, sem=(g.sTm if sp else None))
                kv_done = g.sTm.n

                em.wait("vector", g.sTm, kv_done)
                vdst = vall.rearrange("p (t h c) -> p t h c", t=NT, h=HKV)[:, j, :, 0:64]
                em.op("vector", "tensor_tensor", vdst,
                      psb(512, 512).rearrange("p (h c) -> p h c", h=HKV),
                      biasb[:, 512:1024].rearrange("p (h c) -> p h c", h=HKV),
                      ALU.add, sem=g.sKVe)
                em.wait("vector", g.sX, x_ready)
                em.op("vector", "tensor_tensor", qa, psb(0, 512), biasb[:, 0:512], ALU.add,
                      sem=g.sKVe)
                em.wait("scalar", g.sKVe, g.sKVe.n)
                v3 = qa.rearrange("p (h a c) -> p h a c", h=HKV, a=2)
                r3 = qb.rearrange("p (h a c) -> p h a c", h=HKV, a=2)
                em.op("scalar", "copy", r3[:, :, 0, :], v3[:, :, 1, :])
                em.op("scalar", "copy", r3[:, :, 1, :], v3[:, :, 0, :], sem=g.sLN)
                tabc = tab_[:, pj * 1024:pj * 1024 + 512]
                tabs = tab_[:, pj * 1024 + 512:pj * 1024 + 1024]
                em.op("vector", "tensor_tensor", qc, qa, tabc, ALU.mult)
                em.wait("vector", g.sLN, g.sLN.n)
                em.op("vector", "tensor_tensor", qd, qb, tabs, ALU.mult)
                em.op("vector", "tensor_tensor", kropeb, qc, qd, ALU.add, sem=g.sKVe)
                tabk_used[j] = g.sKVe.n
                em.wait("tensor", g.sKVe, tabk_used[j])
                pkb = psbf(2048, 256)
                for i in range(4):
                    em.op("tensor", "transpose", pkb[:, i * P:(i + 1) * P],
                          kropeb[:, i * P:(i + 1) * P], identb, sem=(g.sTm if i == 3 else None))
                em.wait("vector", g.sTm, g.sTm.n)
                kde = kaugT.rearrange("p (h c) -> p h c", h=HKV)
                em.op("vector", "tensor_copy", kde[0:64, 0::2, j * P:(j + 1) * P],
                      pkb[0:64, :].rearrange("p (i c) -> p i c", i=4))
                em.op("vector", "tensor_copy", kde[0:64, 1::2, j * P:(j + 1) * P],
                      pkb[64:128, :].rearrange("p (i c) -> p i c", i=4), sem=g.sKVe)
                em.op("vector", "tensor_tensor", sqb, kropeb, kropeb, ALU.mult)
                em.op("vector", "reduce_sum",
                      normk[:, j * 8:(j + 1) * 8].unsqueeze(2),
                      sqb.rearrange("p (h c) -> p h c", h=HKV), axis=AX.X, sem=g.sVs)
                em.wait("vector", g.sVs, g.sVs.n)
                em.op("vector", "tensor_tensor", kacc[:, :], kacc[:, :],
                      normk[:, j * 8:(j + 1) * 8], ALU.max, sem=g.sKVe)
                kvevac_done[j] = g.sKVe.n

            kv_all_done = g.sKVe.n
            kv_mm_done = g.sTm.n

            em.wait("vector", g.sKVe, g.sKVe.n)
            em.op("vector", "tensor_copy", lnst[:, 0:8], kacc[:, :], sem=g.sMean)
            em.wait("tensor", g.sMean, g.sMean.n)
            em.wait("tensor", g.sKVe, kv_all_done)
            em.op("tensor", "transpose", psb(2048, 128)[0:8, :], lnst[:, 0:8], identf[:, :], sem=g.sTm)
            em.wait("scalar", g.sTm, g.sTm.n)
            em.op("scalar", "copy", kmrow[0:8, :], psb(2048, 128)[0:8, 0:P], sem=g.sLN)
            em.wait("vector", g.sLN, g.sLN.n)
            em.op("vector", "reduce_max", km8[:, 0:1], kmrow[0:8, :], axis=AX.X, sem=g.sVs)
            em.wait("vector", g.sVs, g.sVs.n)
            em.op("vector", "tensor_scalar_mul", km8[:, :], km8[:, :], 1.0609, sem=g.sMean)
            em.wait("gpsimd", g.sMean, g.sMean.n)
            em.op("gpsimd", "dma_start", out=kmrow2[:, :], in_=km8[:, :], sem=g.sB, k=16)
            em.wait("vector", g.sB, g.sB.n)
            for jj in range(4):
                em.op("vector", "tensor_copy",
                      km32.rearrange("p (k j2) -> p k j2", j2=4)[:, :, jj], kmrow2[:, :],
                      sem=(g.sMean if jj == 3 else None))
            em.wait("tensor", g.sMean, g.sMean.n)
            em.op("tensor", "matmul", psb(2048, 32), onesf[0:1, :], km32[0:1, :],
                  start=True, stop=True, skip_group_check=True, sem=g.sTm)
            em.wait("scalar", g.sTm, g.sTm.n)
            em.op("scalar", "copy", kmax2b[:, :], psb(2048, 32), sem=g.sLN)
            kmax_ready = g.sLN.n

            em.wait("sync", g.sKVe, kv_all_done)
            em.op("sync", "dma_start", out=biasb[:, :],
                  in_=dbq.unsqueeze(0).partition_broadcast(P), sem=g.sX, k=16)
            bq_ready = g.sX.n

            prev_chunk_S_done = None
            prev_chunk_out_done = None
            att_pi = 0
            pair_e = {}
            yt_hist = {}
            av_full = {}

            for ci, ch in enumerate(chunks):
                b0 = ch * NB
                njt = b0 + NB

                qgrp_done = {}
                rope_done = {}
                if ci == 1:
                    em.wait("vector", g.sS, prev_chunk_S_done)
                    em.wait("gpsimd", g.sS, prev_chunk_S_done)
                    em.wait("vector", g.sOutW, prev_chunk_out_done)
                tr_done = {}
                evac_hist = []
                for ecg in range(2):
                    for dt in range(16):
                        sl = dt % 2
                        if (ecg * 16 + dt) >= 2:
                            key = (ecg, dt - 2) if dt >= 2 else (ecg - 1, 14 + dt)
                            em.wait("gpsimd", g.sTm, qgrp_done[key])
                        else:
                            em.wait("gpsimd", g.sTm, g.sTm.n)
                        em.op("gpsimd", "dma_start", out=wslab[sl],
                              in_=dwq[dt * P:(dt + 1) * P, ecg * 1024:(ecg + 1) * 1024],
                              sem=g.sW, k=16)
                        em.wait("tensor", g.sW, g.sW.n)
                        if dt == 0:
                            em.wait("tensor", g.sX, bq_ready)
                            if ecg == 1:
                                em.wait("tensor", g.sQT, g.sQT.n)
                            if ci == 0 and ecg == 0:
                                em.wait("tensor", g.sLN, kmax_ready)
                            if ci == 1 and ecg == 0:
                                em.wait("tensor", g.sRB, g.sRB.n)
                                em.wait("tensor", g.sYT, g.sYT.n)
                                em.wait("tensor", g.sOv, g.sOv.n)
                        for e2 in range(2):
                            for bi in range(4):
                                bank = e2 * 4 + bi
                                em.op("tensor", "matmul", psb(bank * 512, 512),
                                      xnT[:, dt * 1024 + (ci * 4 + bi) * P:
                                          dt * 1024 + (ci * 4 + bi) * P + P],
                                      wslab[sl][:, e2 * 512:(e2 + 1) * 512],
                                      start=(dt == 0), stop=(dt == 15),
                                      sem=(g.sTm if (e2 == 1 and bi == 3) else None))
                        qgrp_done[(ecg, dt)] = g.sTm.n
                    qmm = g.sTm.n
                    for bi in range(4):
                        blk = b0 + bi
                        if len(evac_hist) >= 2:
                            em.wait("sync", g.sRope, evac_hist[-2])
                        em.op("sync", "dma_start",
                              out=tab_[:, (bi % 2) * 1024:(bi % 2 + 1) * 1024],
                              in_=dtq[blk * P:(blk + 1) * P, :], sem=g.sX, k=16)
                        tabq_ready = g.sX.n
                        for e2 in range(2):
                            ec = ecg * 2 + e2
                            bank = e2 * 4 + bi
                            em.wait("vector", g.sTm, qmm)
                            em.op("vector", "tensor_tensor", qa, psb(bank * 512, 512),
                                  biasb[:, ec * 512:(ec + 1) * 512], ALU.add, sem=g.sRope)
                            em.wait("scalar", g.sRope, g.sRope.n)
                            v3 = qa.rearrange("p (h a c) -> p h a c", h=8, a=2)
                            r3 = qb.rearrange("p (h a c) -> p h a c", h=8, a=2)
                            em.op("scalar", "copy", r3[:, :, 0, :], v3[:, :, 1, :])
                            em.op("scalar", "copy", r3[:, :, 1, :], v3[:, :, 0, :], sem=g.sLN)
                            em.wait("vector", g.sX, tabq_ready)
                            tq_c = tab_[:, (bi % 2) * 1024:(bi % 2) * 1024 + 512]
                            tq_s = tab_[:, (bi % 2) * 1024 + 512:(bi % 2) * 1024 + 1024]
                            em.op("vector", "tensor_tensor", qc, qa, tq_c, ALU.mult)
                            em.wait("vector", g.sLN, g.sLN.n)
                            rslot = ropeb[:, (bank % 2) * 512:(bank % 2) * 512 + 512]
                            if len(evac_hist) >= 2:
                                em.wait("vector", g.sTm, tr_done[evac_hist[-2]])
                            em.op("vector", "tensor_tensor", qd, qb, tq_s, ALU.mult)
                            em.op("vector", "tensor_tensor", rslot, qc, qd, ALU.add, sem=g.sRope)
                            rope_ms = g.sRope.n
                            em.op("vector", "tensor_tensor", sqb, rslot, rslot, ALU.mult)
                            em.op("vector", "reduce_sum",
                                  normsq.rearrange("p (b c) -> p b c", b=4)[:, bi,
                                                                            ec * 8:(ec + 1) * 8].unsqueeze(2),
                                  sqb.rearrange("p (h c) -> p h c", h=8), axis=AX.X, sem=g.sRope)
                            em.wait("tensor", g.sRope, rope_ms)
                            ptq = psbf(bank * 512, 256)
                            for i in range(4):
                                em.op("tensor", "transpose", ptq[:, i * P:(i + 1) * P],
                                      rslot[:, i * P:(i + 1) * P], identb,
                                      sem=(g.sTm if i == 3 else None))
                            tr_done[rope_ms] = g.sTm.n
                            evac_hist.append(rope_ms)
                            em.wait("vector", g.sTm, g.sTm.n)
                            qde = qaugT.rearrange("p (gh c) -> p gh c", gh=HQ)
                            em.op("vector", "tensor_copy",
                                  qde[0:64, 8 * ec + 0:8 * ec + 8:2, bi * P:(bi + 1) * P],
                                  ptq[0:64, :].rearrange("p (i c) -> p i c", i=4))
                            em.op("vector", "tensor_copy",
                                  qde[0:64, 8 * ec + 1:8 * ec + 8:2, bi * P:(bi + 1) * P],
                                  ptq[64:128, :].rearrange("p (i c) -> p i c", i=4), sem=g.sQT)

                em.wait("vector", g.sLN, kmax_ready)
                em.wait("tensor", g.sQT, g.sQT.n)
                for bi in range(4):
                    nsq = normsq.rearrange("p (b c) -> p b c", b=4)[:, bi, 0:32]
                    em.op("vector", "tensor_tensor", qc[:, 0:32], nsq, kmax2b[:, :], ALU.mult,
                          sem=g.sMean)
                    em.wait("scalar", g.sMean, g.sMean.n)
                    em.op("scalar", "activation", qd[:, 0:32], qc[:, 0:32],
                          AF.Sqrt, sem=g.sLN)
                    em.wait("tensor", g.sLN, g.sLN.n)
                    em.wait("tensor", g.sMean, g.sMean.n)
                    em.op("tensor", "transpose", psb(2048, 128)[0:32, :], qd[:, 0:32], identf[:, :],
                          sem=g.sTm)
                    em.wait("vector", g.sTm, g.sTm.n)
                    em.wait("vector", g.sB, g.sB.n)
                    em.op("vector", "tensor_copy", bTb[0:32, :], psb(2048, 128)[0:32, 0:P],
                          sem=g.sMean)
                    em.wait("gpsimd", g.sMean, g.sMean.n)
                    em.op("gpsimd", "dma_start",
                          out=qaugT[64:65, :].rearrange("p (gh c) -> p gh c", gh=HQ)[:, :,
                                                                                     bi * P:(bi + 1) * P],
                          in_=bTb[0:32, :], sem=g.sB, k=16)

                if debug:
                    em.wait("gpsimd", g.sQT, g.sQT.n)
                    em.op("gpsimd", "dma_start", out=ddbg["qaug2"][ci], in_=qaugT, sem=g.sCast, k=16)
                em.wait("tensor", g.sQT, g.sQT.n)
                em.wait("tensor", g.sB, g.sB.n)
                em.wait("tensor", g.sKVe, kv_all_done)

                def epi_sc(pi):
                    p = pi % 2
                    avb = 2048 + p * 1024
                    em.wait("scalar", g.sAV, g.sAV.n)
                    if pi - 2 in yt_hist:
                        em.wait("scalar", g.sYT, yt_hist[pi - 2])
                    em.op("scalar", "activation", rbf2[0:1, p * 1024:p * 1024 + 1024],
                          ps[64:65, avb:avb + 1024], AF.Ln, sem=g.sRC)
                    em.wait("scalar", g.sRC, g.sRC.n)
                    em.op("scalar", "activation", rb[0:1, p * 1024:p * 1024 + 1024],
                          rbf2[0:1, p * 1024:p * 1024 + 1024], AF.Exp, scale=-1.0, sem=g.sRC)
                    em.wait("scalar", g.sRC, g.sRC.n)
                    em.op("scalar", "copy", rbb[0:1, p * 1024:p * 1024 + 1024],
                          rb[0:1, p * 1024:p * 1024 + 1024], sem=g.sRC)

                def epi_bcast(pi):
                    p = pi % 2
                    avb = 2048 + p * 1024
                    em.wait("tensor", g.sRC, g.sRC.n)
                    for c in range(2):
                        em.op("tensor", "matmul",
                              ps[64:128, avb + c * 512: avb + (c + 1) * 512],
                              onesb[0:1, 0:64],
                              rbb[0:1, p * 1024 + c * 512: p * 1024 + (c + 1) * 512],
                              start=True, stop=True, skip_group_check=True,
                              tile_position=(0, 64),
                              sem=(g.sBC if c == 1 else None))

                def epi_norm(pi, gg, h):
                    p = pi % 2
                    avb = 2048 + p * 1024
                    em.wait("scalar", g.sBC, g.sBC.n)
                    if pi - 2 in yt_hist:
                        em.wait("scalar", g.sYT, yt_hist[pi - 2])
                    em.op("scalar", "copy", rb[0:64, p * 1024:p * 1024 + 1024],
                          ps[64:128, avb:avb + 1024], sem=g.sRB)
                    em.wait("vector", g.sRB, g.sRB.n)
                    for c in range(2):
                        gq = gg * 4 + 2 * h + c
                        em.op("vector", "tensor_tensor",
                              yt[(gq % 2) * 64:(gq % 2) * 64 + 64,
                                 (gq // 2) * 512:(gq // 2) * 512 + 512],
                              ps[0:64, avb + c * 512: avb + (c + 1) * 512],
                              rb[0:64, p * 1024 + c * 512: p * 1024 + (c + 1) * 512],
                              ALU.mult, sem=(g.sYT if c == 1 else None))
                    yt_hist[pi] = g.sYT.n

                pend = None
                for gg in range(8):
                    kh = gg
                    for h in range(2):
                        pi = att_pi
                        p = pi % 2
                        avb = 2048 + p * 1024
                        av_pass = {}

                        def emit_av(jt, i, q0, m_cnt, e_cnt):
                            pe = jt % 2
                            if i >= 0:
                                em.wait("tensor", g.sM, m_cnt)
                            else:
                                em.wait("tensor", g.sE, e_cnt)
                            if jt == 0 and pi - 2 in yt_hist:
                                em.wait("tensor", g.sYT, yt_hist[pi - 2])
                            for c in range(2):
                                s4 = 2 * h + c
                                em.op("tensor", "matmul",
                                      ps[0:65, avb + c * 512 + q0: avb + (c + 1) * 512],
                                      vall[0:128, jt * 520 + kh * 65: jt * 520 + kh * 65 + 65],
                                      e4[pe][:, s4 * 512 + q0: (s4 + 1) * 512],
                                      start=(jt == 0), stop=(jt == njt - 1), skip_group_check=True,
                                      sem=(g.sAV if c == 1 else None))
                            av_pass[jt] = g.sAV.n

                        av_prev = None
                        for jt in range(njt):
                            i = jt - b0
                            q0 = max(0, i) * P
                            pe = jt % 2
                            sbase = pe * 1024
                            if pe in pair_e:
                                em.wait("tensor", g.sE, pair_e[pe])
                            for c in range(2):
                                gq = gg * 4 + 2 * h + c
                                em.op("tensor", "matmul",
                                      ps[0:128, sbase + c * 512 + q0: sbase + (c + 1) * 512],
                                      kaugT[0:65, kh * 2048 + jt * P: kh * 2048 + (jt + 1) * P],
                                      qaugT[0:65, gq * 512 + q0: (gq + 1) * 512],
                                      start=True, stop=True, skip_group_check=True,
                                      sem=(g.sS if c == 1 else None))
                            s_cnt = g.sS.n
                            if jt == 0 and pend is not None:
                                epi_bcast(pend[0])
                            em.wait("scalar", g.sS, s_cnt)
                            if jt >= 2:
                                em.wait("scalar", g.sAV, av_pass[jt - 2])
                            elif pi - 2 in av_full:
                                em.wait("scalar", g.sAV, av_full[pi - 2])
                            eview = e4[pe].rearrange("p (s c) -> p s c", s=4)[:, 2 * h:2 * h + 2, q0:512]
                            sview = ps[:, sbase:sbase + 1024].rearrange("p (s c) -> p s c", s=2)[:, :, q0:512]
                            em.op("scalar", "activation", eview, sview, AF.Exp, sem=g.sE)
                            e_cnt = g.sE.n
                            pair_e[pe] = e_cnt
                            m_cnt = 0
                            if i >= 0:
                                em.wait("vector", g.sE, e_cnt)
                                for c in range(2):
                                    s4 = 2 * h + c
                                    em.op("vector", "tensor_tensor",
                                          e4[pe][:, s4 * 512 + q0: s4 * 512 + q0 + P],
                                          e4[pe][:, s4 * 512 + q0: s4 * 512 + q0 + P],
                                          maskb, ALU.mult,
                                          sem=(g.sM if c == 1 else None))
                                m_cnt = g.sM.n
                            if jt == 0 and pend is not None:
                                epi_norm(*pend)
                                pend = None
                            if av_prev is not None:
                                emit_av(*av_prev)
                            av_prev = (jt, i, q0, m_cnt, e_cnt)
                        emit_av(*av_prev)
                        av_full[pi] = g.sAV.n
                        epi_sc(pi)
                        pend = (pi, gg, h)
                        att_pi += 1
                epi_bcast(pend[0])
                epi_norm(*pend)
                pend = None
                prev_chunk_S_done = g.sS.n

                em.wait("tensor", g.sYT, g.sYT.n)
                wogrp_done = {}
                ov_hist = []
                outw_hist = []
                for ecg in range(2):
                    for et in range(16):
                        sl = et % 2
                        if (ecg * 16 + et) >= 2:
                            key = (ecg, et - 2) if et >= 2 else (ecg - 1, 14 + et)
                            em.wait("gpsimd", g.sTm, wogrp_done[key])
                        else:
                            em.wait("gpsimd", g.sRB, g.sRB.n)
                            em.wait("gpsimd", g.sQT, g.sQT.n)
                        em.op("gpsimd", "dma_start", out=wslab[sl],
                              in_=dwo[et * P:(et + 1) * P, ecg * 1024:(ecg + 1) * 1024],
                              sem=g.sW, k=16)
                        em.wait("tensor", g.sW, g.sW.n)
                        if et == 0:
                            if ecg == 1:
                                em.wait("tensor", g.sOv, g.sOv.n)
                            else:
                                em.wait("tensor", g.sRB, g.sRB.n)
                                em.wait("tensor", g.sYT, g.sYT.n)
                        for e2 in range(2):
                            for bi in range(4):
                                bank = e2 * 4 + bi
                                em.op("tensor", "matmul", psb(bank * 512, 512),
                                      yt[:, et * 512 + bi * P: et * 512 + (bi + 1) * P],
                                      wslab[sl][:, e2 * 512:(e2 + 1) * 512],
                                      start=(et == 0), stop=False, skip_group_check=True,
                                      sem=(g.sTm if (e2 == 1 and bi == 3) else None))
                        wogrp_done[(ecg, et)] = g.sTm.n
                    for e2 in range(2):
                        ec = ecg * 2 + e2
                        for bi in range(4):
                            bank = e2 * 4 + bi
                            blk = b0 + bi
                            em.op("tensor", "matmul", psb(bank * 512, 512),
                                  onesb[0:1, :], bob2[0:1, ec * 512:(ec + 1) * 512],
                                  start=False, stop=True, skip_group_check=True, sem=g.sTm)
                            womm = g.sTm.n
                            inst = len(ov_hist)
                            if inst >= 4:
                                em.wait("sync", g.sOv, ov_hist[-4])
                            else:
                                em.wait("sync", g.sB, g.sB.n)
                            em.op("sync", "dma_start", out=xr4[inst % 4],
                                  in_=dx[blk * P:(blk + 1) * P, ec * 512:(ec + 1) * 512],
                                  sem=g.sX, k=16)
                            em.wait("vector", g.sX, g.sX.n)
                            em.wait("vector", g.sTm, womm)
                            if len(outw_hist) >= 2:
                                em.wait("vector", g.sOutW, outw_hist[-2])
                            em.op("vector", "tensor_tensor", qa if inst % 2 == 0 else qb,
                                  psb(bank * 512, 512), xr4[inst % 4], ALU.add, sem=g.sOv)
                            ov_hist.append(g.sOv.n)
                            em.wait("sync", g.sOv, g.sOv.n)
                            em.op("sync", "dma_start",
                                  out=dout[(ci * 4 + bi) * P:(ci * 4 + bi + 1) * P,
                                           ec * 512:(ec + 1) * 512],
                                  in_=qa if inst % 2 == 0 else qb, sem=g.sOutW, k=16)
                            outw_hist.append(g.sOutW.n)
                prev_chunk_out_done = g.sOutW.n

            all_attn_S = g.sS.n
            all_out_w = g.sOutW.n

            if debug:
                em.wait("gpsimd", g.sOutW, all_out_w)
                for _src, _dst in ((xnT, ddbg["xnT"]), (kaugT, ddbg["kaug"]), (vall, ddbg["vall"]),
                                   (qaugT, ddbg["qaug"]), (yt, ddbg["yt"])):
                    em.op("gpsimd", "dma_start", out=_dst, in_=_src, sem=g.sCast, k=16)
                em.op("gpsimd", "dma_start", out=ddbg["misc"][:, 0:32], in_=kmax2b[:, :], sem=g.sCast, k=16)
                em.op("gpsimd", "dma_start", out=ddbg["misc"][:, 32:160], in_=normsq[:, :], sem=g.sCast, k=16)
                em.wait("gpsimd", g.sCast, g.sCast.n)

            em.wait("gpsimd", g.sTm, g.sTm.n)
            em.wait("gpsimd", g.sS, all_attn_S)
            em.wait("vector", g.sAV, g.sAV.n)
            em.wait("vector", g.sS, all_attn_S)
            em.wait("vector", g.sOutW, all_out_w)
            em.wait("scalar", g.sOutW, all_out_w)
            em.wait("scalar", g.sS, all_attn_S)
            em.wait("scalar", g.sYT, g.sYT.n)
            ffn_grp_done = {}
            ga_hist = []
            de_hist = []
            accum_hist = []
            for hi in range(2):
                for hg in range(32):
                    sl = hg % 2
                    if (hi, hg - 2) in ffn_grp_done or (hi - 1, 30 + hg) in ffn_grp_done:
                        prevk = (hi, hg - 2) if hg >= 2 else (hi - 1, 30 + hg)
                        em.wait("gpsimd", g.sTm, ffn_grp_done[prevk])
                    em.op("gpsimd", "dma_start", out=wgs[sl], in_=dwg[hg],
                          sem=g.sW, k=16)
                    em.op("gpsimd", "dma_start", out=wus[sl], in_=dwu[hg],
                          sem=g.sW, k=16)
                    em.wait("tensor", g.sW, g.sW.n)
                    for h2 in range(2):
                        ht = hg * 2 + h2
                        gbk = (ht % 2) * 2
                        ubk = gbk + 1
                        if len(ga_hist) >= 2:
                            em.wait("tensor", g.sGa, ga_hist[-2])
                        elif hi == 0 and hg == 0 and h2 == 0:
                            em.wait("tensor", g.sOv, g.sOv.n)
                        for dt in range(16):
                            em.op("tensor", "matmul", psb(gbk * 512, 512),
                                  wgs[sl][:, dt * 256 + h2 * P: dt * 256 + (h2 + 1) * P],
                                  xnT[:, dt * 1024 + hi * 512: dt * 1024 + (hi + 1) * 512],
                                  start=(dt == 0), stop=(dt == 15),
                                  sem=(g.sTm if dt == 15 else None))
                        for dt in range(16):
                            em.op("tensor", "matmul", psb(ubk * 512, 512),
                                  wus[sl][:, dt * 256 + h2 * P: dt * 256 + (h2 + 1) * P],
                                  xnT[:, dt * 1024 + hi * 512: dt * 1024 + (hi + 1) * 512],
                                  start=(dt == 0), stop=(dt == 15),
                                  sem=(g.sTm if dt == 15 else None))
                        em.wait("scalar", g.sTm, g.sTm.n)
                        em.op("scalar", "activation", qd, psb(gbk * 512, 512), AF.Silu, sem=g.sSi)
                        em.wait("vector", g.sSi, g.sSi.n)
                        em.op("vector", "tensor_tensor", gated[:, ht * 512:(ht + 1) * 512],
                              qd, psb(ubk * 512, 512), ALU.mult, sem=g.sGa)
                        ga_hist.append(g.sGa.n)
                    ffn_grp_done[(hi, hg)] = g.sTm.n
                em.wait("tensor", g.sGa, g.sGa.n)
                wd_done = {}
                for ec in range(4):
                    for hgrp in range(8):
                        sl = hgrp % 2
                        if (ec, hgrp - 2) in wd_done or (ec - 1, 6 + hgrp) in wd_done:
                            prevk = (ec, hgrp - 2) if hgrp >= 2 else (ec - 1, 6 + hgrp)
                            em.wait("gpsimd", g.sTm, wd_done[prevk])
                        em.op("gpsimd", "dma_start", out=wds[sl], in_=dwd[ec, hgrp],
                              sem=g.sW, k=16)
                        em.wait("tensor", g.sW, g.sW.n)
                        if hgrp == 0 and len(de_hist) >= 4:
                            em.wait("tensor", g.sDe, de_hist[-4])
                        for hh in range(8):
                            ht = hgrp * 8 + hh
                            for bi in range(4):
                                em.op("tensor", "matmul", psb((4 + bi) * 512, 512),
                                      gated[:, ht * 512 + bi * P: ht * 512 + (bi + 1) * P],
                                      wds[sl][:, hh * 512:(hh + 1) * 512],
                                      start=(ht == 0), stop=(ht == 63), skip_group_check=True,
                                      sem=(g.sTm if (hh == 7 and bi == 3) else None))
                        wd_done[(ec, hgrp)] = g.sTm.n
                    em.wait("scalar", g.sTm, g.sTm.n)
                    for bi in range(4):
                        inst = len(de_hist)
                        if inst >= 2:
                            em.wait("scalar", g.sOutF, accum_hist[-2])
                        em.op("scalar", "copy", qa if inst % 2 == 0 else qb,
                              psb((4 + bi) * 512, 512), sem=g.sDe)
                        de_hist.append(g.sDe.n)
                        em.wait("gpsimd", g.sDe, g.sDe.n)
                        if len(accum_hist) == 0:
                            em.wait("gpsimd", g.sOutW, all_out_w)
                        if ffn_split:
                            em.op("gpsimd", "dma_start",
                                  out=dffn[(hi * 4 + bi) * P:(hi * 4 + bi + 1) * P,
                                           ec * 512:(ec + 1) * 512],
                                  in_=qa if inst % 2 == 0 else qb,
                                  sem=g.sOutF, k=16)
                        else:
                            em.op("gpsimd", "dma_start",
                                  out=dout[(hi * 4 + bi) * P:(hi * 4 + bi + 1) * P,
                                           ec * 512:(ec + 1) * 512],
                                  in_=qa if inst % 2 == 0 else qb, accum_op=ALU.add,
                                  sem=g.sOutF, k=16)
                        accum_hist.append(g.sOutF.n)

            em.wait("gpsimd", g.sOutF, g.sOutF.n)
            em.wait("gpsimd", g.sOutW, g.sOutW.n)
            em.wait("sync", g.sOutF, g.sOutF.n)
            em.wait("sync", g.sOutW, g.sOutW.n)

        def emit_engine(name):
            eng = getattr(nc, name)
            for fn in em.pre[name]:
                fn()
            eng.wait_ge(sems["sFlag"].h, 16)
            with eng.register("r_" + name) as r:
                eng.reg_load(r, fl[0:1, 0:1])
                with eng.If_eq(r, 0):
                    for fn in em.bod[0][name]:
                        fn()
                with eng.Else():
                    for fn in em.bod[1][name]:
                        fn()

        @block.sync
        def _(s):
            emit_engine("sync")

        @block.tensor
        def _(t):
            emit_engine("tensor")

        @block.vector
        def _(v):
            emit_engine("vector")

        @block.scalar
        def _(s):
            emit_engine("scalar")

        @block.gpsimd
        def _(gp):
            emit_engine("gpsimd")

    return nc


def _host_inputs(inputs):
    x = np.asarray(inputs["L0_attn_x"], np.float32)
    wq = np.ascontiguousarray(np.asarray(inputs["L0_attn_Wq"], np.float32))
    wk = np.ascontiguousarray(np.asarray(inputs["L0_attn_Wk"], np.float32))
    wv = np.ascontiguousarray(np.asarray(inputs["L0_attn_Wv"], np.float32))
    wo = np.ascontiguousarray(np.asarray(inputs["L0_attn_Wo"], np.float32))
    bq = np.ascontiguousarray(np.asarray(inputs["L0_attn_bq"], np.float32))
    bk = np.ascontiguousarray(np.asarray(inputs["L0_attn_bk"], np.float32))
    bv = np.ascontiguousarray(np.asarray(inputs["L0_attn_bv"], np.float32))
    bo = np.ascontiguousarray(np.asarray(inputs["L0_attn_bo"], np.float32))
    wg = np.asarray(inputs["L0_ffn_w_gate"], np.float32)
    wu = np.asarray(inputs["L0_ffn_w_up"], np.float32)
    wd = np.asarray(inputs["L0_ffn_w_down"], np.float32)

    wg2 = np.ascontiguousarray(wg.reshape(32, 256, 16, 128).transpose(0, 3, 2, 1).reshape(32, 128, 4096))
    wu2 = np.ascontiguousarray(wu.reshape(32, 256, 16, 128).transpose(0, 3, 2, 1).reshape(32, 128, 4096))
    wd4 = np.ascontiguousarray(
        wd.reshape(4, 512, 8, 8, 128).transpose(0, 2, 4, 3, 1).reshape(4, 8, 128, 4096))

    pos = np.arange(1, T + 1, dtype=np.float64)[:, None]
    freqs = 10000.0 ** (-(2.0 * np.arange(D // 2, dtype=np.float64)) / D)
    th = pos * freqs
    cos = np.cos(th)
    sin = np.sin(th)
    Ct = np.concatenate([cos, cos], 1)
    St = np.concatenate([-sin, sin], 1)
    Crep = np.tile(Ct, (1, 8))
    Srep = np.tile(St, (1, 8))
    tk = np.ascontiguousarray(np.concatenate([Crep, Srep], 1).astype(np.float32))
    tq = np.ascontiguousarray((np.concatenate([Crep, Srep], 1) * 0.125).astype(np.float32))
    identm = np.eye(P, dtype=np.float32)
    maskm = np.ascontiguousarray(np.triu(np.ones((P, P), np.float32)))

    in_maps = []
    for c in range(8):
        b, par = c // 2, c % 2
        in_maps.append({
            "x": np.ascontiguousarray(x[b]),
            "wq": wq, "wk": wk, "wv": wv, "wo": wo,
            "bq": bq, "bk": bk, "bv": bv, "bo": bo,
            "wg2": wg2, "wu2": wu2, "wd4": wd4,
            "tq": tq, "tk": tk, "identm": identm, "maskm": maskm,
            "flag": np.array([par], np.int32),
        })
    return in_maps


_NC_CACHE = {}


def kernel(**inputs):
    in_maps = _host_inputs(inputs)
    if "nc" not in _NC_CACHE:
        _NC_CACHE["nc"] = build_nc(debug=False)
    nc = _NC_CACHE["nc"]
    res = run_bass_kernel_spmd(nc, in_maps, core_ids=list(range(8)))
    out = np.zeros((4, T, E), np.float32)
    for c in range(8):
        b, par = c // 2, c % 2
        o = np.asarray(res.results[c]["out"], np.float32)
        lo, hi = CHUNKS[par]
        out[b, lo * 512:(lo + 1) * 512] = o[0:512]
        out[b, hi * 512:(hi + 1) * 512] = o[512:1024]
    return out


# version 28
# speedup vs baseline: 1.1717x; 1.0468x over previous
import contextlib
import numpy as np
import ml_dtypes
import concourse.bass as bass
import concourse.mybir as mybir
from concourse.bass_utils import run_bass_kernel_spmd

F32 = mybir.dt.float32
BF16 = mybir.dt.bfloat16
I32 = mybir.dt.int32
AF = mybir.ActivationFunctionType
ALU = mybir.AluOpType
AX = mybir.AxisListType

P = 128
T = 2048
E = 2048
DKV = 512
HQ, HKV, D = 32, 8, 64
NT = 16
NB = 4
EPS = 1e-5

CHUNKS = {0: (0, 3), 1: (1, 2)}
NKV = {0: 16, 1: 12}

SEM_NAMES = [
    "sFlag", "sX", "sW", "sLN", "sMean", "sXP", "sTm", "sKVe", "sRope",
    "sQT", "sB", "sS", "sE", "sM", "sAV", "sRC", "sBC", "sRB", "sYT",
    "sOv", "sOutW", "sOutF", "sSi", "sGa", "sDe", "sVs", "sCast",
]


class Sem:
    def __init__(self, h):
        self.h = h
        self.n = 0


class Emitter:
    ENGS = ("sync", "tensor", "vector", "scalar", "gpsimd")

    def __init__(self, nc):
        self.nc = nc
        self.pre = {e: [] for e in self.ENGS}
        self.bod = [{e: [] for e in self.ENGS}, {e: [] for e in self.ENGS}]
        self.cur = None

    def _q(self, eng):
        return self.pre[eng] if self.cur is None else self.bod[self.cur][eng]

    def op(self, eng, meth, *args, sem=None, k=1, **kw):
        nc = self.nc

        def fn(eng=eng, meth=meth, args=args, kw=kw, sem=sem, k=k):
            ins = getattr(getattr(nc, eng), meth)(*args, **kw)
            if sem is not None:
                ins.then_inc(sem.h, k)
        if sem is not None:
            sem.n += k
        self._q(eng).append(fn)
        return sem.n if sem is not None else None

    def wait(self, eng, sem, val=None):
        nc = self.nc
        v = sem.n if val is None else val
        if v <= 0:
            return

        def fn(eng=eng, sem=sem, v=v):
            getattr(nc, eng).wait_ge(sem.h, v)
        self._q(eng).append(fn)


def build_nc(debug=False, ffn_split=False):
    nc = bass.Bass("TRN2", target_bir_lowering=False)

    dx = nc.dram_tensor("x", [T, E], F32, kind="ExternalInput").ap()
    dwq = nc.dram_tensor("wq", [E, E], F32, kind="ExternalInput").ap()
    dwk = nc.dram_tensor("wk", [E, DKV], F32, kind="ExternalInput").ap()
    dwv = nc.dram_tensor("wv", [E, DKV], F32, kind="ExternalInput").ap()
    dwo = nc.dram_tensor("wo", [E, E], F32, kind="ExternalInput").ap()
    dbq = nc.dram_tensor("bq", [E], F32, kind="ExternalInput").ap()
    dbk = nc.dram_tensor("bk", [DKV], F32, kind="ExternalInput").ap()
    dbv = nc.dram_tensor("bv", [DKV], F32, kind="ExternalInput").ap()
    dbo = nc.dram_tensor("bo", [E], F32, kind="ExternalInput").ap()
    dwg = nc.dram_tensor("wg2", [32, 128, 4096], F32, kind="ExternalInput").ap()
    dwu = nc.dram_tensor("wu2", [32, 128, 4096], F32, kind="ExternalInput").ap()
    dwd = nc.dram_tensor("wd4", [4, 8, 128, 4096], F32, kind="ExternalInput").ap()
    dtq = nc.dram_tensor("tq", [T, 1024], F32, kind="ExternalInput").ap()
    dtk = nc.dram_tensor("tk", [T, 1024], F32, kind="ExternalInput").ap()
    dident = nc.dram_tensor("identm", [P, P], F32, kind="ExternalInput").ap()
    dmask = nc.dram_tensor("maskm", [P, P], F32, kind="ExternalInput").ap()
    dflag = nc.dram_tensor("flag", [1], I32, kind="ExternalInput").ap()
    dout = nc.dram_tensor("out", [1024, E], F32, kind="ExternalOutput").ap()
    ddbg = {}
    if debug:
        for nm, w in (("xnT", 16384), ("kaug", 16384), ("vall", 8320),
                      ("qaug", 16384), ("yt", 8192), ("misc", 512)):
            ddbg[nm] = nc.dram_tensor("dbg_" + nm, [P, w], F32, kind="ExternalOutput").ap()
        ddbg["qaug2"] = nc.dram_tensor("dbg_qaug2", [2, P, 16384], F32, kind="ExternalOutput").ap()
    dffn = None
    if ffn_split:
        dffn = nc.dram_tensor("dbg_ffn", [1024, E], F32, kind="ExternalOutput").ap()

    POOLW = 29824

    with contextlib.ExitStack() as stack:
        en = stack.enter_context
        pool = en(nc.sbuf_tensor("pool", [P, POOLW], F32))
        xnT_ = en(nc.sbuf_tensor("xnTw", [P, 8192], F32))
        tab_ = en(nc.sbuf_tensor("tabw", [P, 2048], F32))
        biasb = en(nc.sbuf_tensor("biasb", [P, 2048], F32))
        sg = en(nc.sbuf_tensor("sg", [P, 3072], F32))
        ropebf_ = en(nc.sbuf_tensor("ropew", [P, 512], F32))
        sqbf_ = en(nc.sbuf_tensor("sqw", [P, 256], F32))
        kropebf_ = en(nc.sbuf_tensor("kropew", [P, 256], F32))
        bob_ = en(nc.sbuf_tensor("bobw", [1, 1024], F32))
        rbbf = en(nc.sbuf_tensor("rbbf", [1, 1024], F32))
        rbf2 = en(nc.sbuf_tensor("rbf2", [1, 2048], F32))
        fl = en(nc.sbuf_tensor("fl", [1, 1], I32))
        ident_ = en(nc.sbuf_tensor("identw", [P, 64], F32))
        identf = en(nc.sbuf_tensor("identf", [P, 128], F32))
        mask_ = en(nc.sbuf_tensor("maskw", [P, 64], F32))
        onesb_ = en(nc.sbuf_tensor("onesbw", [1, 64], F32))
        onesf = en(nc.sbuf_tensor("onesf", [1, 128], F32))
        km32 = en(nc.sbuf_tensor("km32", [1, 32], F32))
        kmax2b = en(nc.sbuf_tensor("kmax2b", [P, 32], F32))
        normk = en(nc.sbuf_tensor("normk", [P, 128], F32))
        normsq = en(nc.sbuf_tensor("normsq", [P, 128], F32))
        bTf = en(nc.sbuf_tensor("bTw", [32, 64], F32))
        km8 = en(nc.sbuf_tensor("km8", [8, 1], F32))
        kmrow2 = en(nc.sbuf_tensor("kmrow2", [1, 8], F32))
        kmrow = en(nc.sbuf_tensor("kmrow", [8, 128], F32))
        lnst = en(nc.sbuf_tensor("lnst", [P, 16], F32))
        kacc = en(nc.sbuf_tensor("kacc", [P, 8], F32))
        ps = en(nc.psum_tensor("ps", [P, 4096], F32))
        block = en(nc.Block())
        sems = {nm: Sem(stack.enter_context(nc.semaphore(nm))) for nm in SEM_NAMES}
        g = type("G", (), sems)
        em = Emitter(nc)

        def bfv(t, o, n):
            return t[:, o:o + n].bitcast(BF16)

        xnT = bfv(xnT_, 0, 8192)
        identb = bfv(ident_, 0, 64)
        maskb = bfv(mask_, 0, 64)
        onesb = bfv(onesb_, 0, 64)
        ropeb = bfv(ropebf_, 0, 512)
        sqb = bfv(sqbf_, 0, 256)
        kropeb = bfv(kropebf_, 0, 256)
        bob2 = bob_[0:1, :].bitcast(BF16)
        rbb = rbbf[0:1, :].bitcast(BF16)
        bTb = bTf[0:32, :].bitcast(BF16)

        O_KAUG, O_VALL, O_QWKV, O_XYT, O_WSLAB, O_E4, O_RB = (
            0, 8192, 12352, 20544, 24640, 25664, 27712)
        kaugT = bfv(pool, O_KAUG, 8192)
        vall = bfv(pool, O_VALL, 4160)
        wkb = bfv(pool, O_QWKV, 4096)
        wvb = bfv(pool, O_QWKV + 4096, 4096)
        qaugT = bfv(pool, O_QWKV, 8192)
        xin = [pool[:, O_XYT:O_XYT + 2048], pool[:, O_XYT + 2048:O_XYT + 4096]]
        yt = bfv(pool, O_XYT, 4096)
        wslab = [bfv(pool, O_WSLAB, 512), bfv(pool, O_WSLAB + 512, 512)]
        e4 = [bfv(pool, O_E4, 1024), bfv(pool, O_E4 + 1024, 1024)]
        rb = pool[0:P, O_RB:O_RB + 2048]
        gated = bfv(pool, 0, 16384)
        wgs = [bfv(pool, 16384, 2048), bfv(pool, 18432, 2048)]
        wus = [bfv(pool, 20480, 2048), bfv(pool, 22528, 2048)]
        wds = [bfv(pool, 24576, 2048), bfv(pool, 26624, 2048)]

        def psb(o, n):
            return ps[:, o:o + n]

        def psbf(o, n):
            return ps[:, o:o + n].bitcast(BF16)

        qa = sg[:, 0:512]
        qb = sg[:, 512:1024]
        qc = sg[:, 1024:1536]
        qd = sg[:, 1536:2048]
        xr = [sg[:, 2048:2560], sg[:, 2560:3072]]
        xr4 = [sg[:, 2048:2560], sg[:, 2560:3072], qc, qd]

        em.op("sync", "dma_start", out=fl[:, :], in_=dflag.unsqueeze(0), sem=g.sFlag, k=16)
        em.op("gpsimd", "dma_start", out=identb, in_=dident, sem=g.sSi, k=16)
        em.op("sync", "dma_start", out=identf[:, :], in_=dident, sem=g.sX, k=16)
        em.op("gpsimd", "dma_start", out=maskb, in_=dmask, sem=g.sSi, k=16)
        em.op("gpsimd", "dma_start", out=bob2, in_=dbo.unsqueeze(0), sem=g.sSi, k=16)
        em.op("gpsimd", "dma_start", out=wkb.rearrange("p (i c) -> p i c", i=16),
              in_=dwk.rearrange("(i p) c -> p i c", p=P), sem=g.sCast, k=16)
        em.op("gpsimd", "dma_start", out=wvb.rearrange("p (i c) -> p i c", i=16),
              in_=dwv.rearrange("(i p) c -> p i c", p=P), sem=g.sCast, k=16)
        CAST_WKV = g.sCast.n
        em.op("sync", "dma_start", out=biasb[:, 0:512],
              in_=dbk.unsqueeze(0).partition_broadcast(P), sem=g.sX, k=16)
        em.op("sync", "dma_start", out=biasb[:, 512:1024],
              in_=dbv.unsqueeze(0).partition_broadcast(P), sem=g.sX, k=16)
        em.op("vector", "memset", onesb, 1.0)
        em.op("vector", "memset", onesf[:, :], 1.0)
        em.op("vector", "memset", normk[:, :], 0.0)
        em.op("vector", "memset", kacc[:, :], 0.0)
        em.op("vector", "memset", kaugT[64:65, :], -1.0)
        vones = vall.rearrange("p (t h c) -> p t h c", t=NT, h=HKV)[:, :, :, 64:65]
        em.op("vector", "memset", vones, 1.0, sem=g.sMean)
        pro_counts = {nm: s.n for nm, s in sems.items()}
        PRO_FLAG = g.sFlag.n
        PRO_MEAN = g.sMean.n
        PRO_X = g.sX.n

        for par in (0, 1):
            em.cur = par
            for nm, s in sems.items():
                s.n = pro_counts[nm]
            chunks = CHUNKS[par]
            nkv = NKV[par]
            own_blocks = [c * NB + i for c in chunks for i in range(NB)]

            for eng in Emitter.ENGS:
                em.wait(eng, g.sFlag, PRO_FLAG)
                em.wait(eng, g.sX, PRO_X)
                em.wait(eng, g.sMean, PRO_MEAN)
                em.wait(eng, g.sSi, 48)

            wkv_ready = CAST_WKV

            norm_done = {}
            kvevac_done = {}
            tabk_used = {}
            xn_tr_done = {}

            for j in range(nkv):
                pj = j % 2
                if j >= 2:
                    em.wait("sync", g.sMean, norm_done[j - 2])
                em.op("sync", "dma_start", out=xin[pj], in_=dx[j * P:(j + 1) * P, :], sem=g.sX, k=16)
                if j >= 2:
                    em.wait("sync", g.sKVe, tabk_used[j - 2])
                em.op("sync", "dma_start", out=tab_[:, pj * 1024:(pj + 1) * 1024],
                      in_=dtk[j * P:(j + 1) * P, :], sem=g.sX, k=16)
                x_ready = g.sX.n

                ls = pj * 8
                em.wait("scalar", g.sX, x_ready)
                if j >= 2:
                    em.wait("scalar", g.sMean, norm_done[j - 2])
                em.op("scalar", "activation", e4[1], xin[pj], AF.Square, accum_out=lnst[:, ls + 0:ls + 1])
                em.op("scalar", "activation", e4[1], xin[pj], AF.Copy, accum_out=lnst[:, ls + 1:ls + 2], sem=g.sLN)
                em.wait("vector", g.sLN, g.sLN.n)
                em.op("vector", "tensor_scalar_mul", lnst[:, ls + 1:ls + 2], lnst[:, ls + 1:ls + 2], 1.0 / E)
                em.op("vector", "tensor_scalar_mul", lnst[:, ls + 0:ls + 1], lnst[:, ls + 0:ls + 1], 1.0 / E, sem=g.sVs)
                em.wait("vector", g.sVs, g.sVs.n)
                em.op("vector", "tensor_tensor", lnst[:, ls + 2:ls + 3], lnst[:, ls + 1:ls + 2], lnst[:, ls + 1:ls + 2],
                      ALU.mult, sem=g.sVs)
                em.wait("vector", g.sVs, g.sVs.n)
                em.op("vector", "tensor_tensor", lnst[:, ls + 0:ls + 1], lnst[:, ls + 0:ls + 1], lnst[:, ls + 2:ls + 3],
                      ALU.subtract, sem=g.sVs)
                em.wait("vector", g.sVs, g.sVs.n)
                em.op("vector", "tensor_scalar_add", lnst[:, ls + 0:ls + 1], lnst[:, ls + 0:ls + 1], EPS, sem=g.sMean)
                em.wait("scalar", g.sMean, g.sMean.n)
                em.op("scalar", "activation", lnst[:, ls + 3:ls + 4], lnst[:, ls + 0:ls + 1], AF.Sqrt, sem=g.sLN)
                em.wait("vector", g.sLN, g.sLN.n)
                em.op("vector", "reciprocal", lnst[:, ls + 4:ls + 5], lnst[:, ls + 3:ls + 4], sem=g.sVs)
                em.wait("vector", g.sVs, g.sVs.n)
                xnb = e4[0]
                if j >= 1:
                    em.wait("vector", g.sTm, xn_tr_done[j - 1])
                em.op("vector", "tensor_scalar", xnb, xin[pj], lnst[:, ls + 1:ls + 2], lnst[:, ls + 4:ls + 5],
                      ALU.subtract, ALU.mult, sem=g.sMean)
                norm_done[j] = g.sMean.n

                own = j in own_blocks
                owncol = own_blocks.index(j) if own else -1

                em.wait("tensor", g.sMean, norm_done[j])
                if j >= 1:
                    em.wait("tensor", g.sKVe, kvevac_done[j - 1])
                xp_round = {}
                for r in range(4):
                    pb = psbf(1024 + (r % 2) * 256, 256)
                    if (j, r - 2) in xp_round or (j - 1, r + 2) in xp_round:
                        prev = xp_round.get((j, r - 2), xp_round.get((j - 1, r + 2)))
                        em.wait("tensor", g.sXP, prev)
                    for i in range(4):
                        dt = r * 4 + i
                        em.op("tensor", "transpose", pb[:, i * P:(i + 1) * P],
                              xnb[:, dt * P:(dt + 1) * P], identb, sem=(g.sTm if i == 3 else None))
                    if r == 3:
                        xn_tr_done[j] = g.sTm.n
                    em.wait("vector", g.sTm, g.sTm.n)
                    if own:
                        dst = xnT.rearrange("p (d c) -> p d c", d=16)[:, r * 4:r * 4 + 4,
                                                                     owncol * P:(owncol + 1) * P]
                    else:
                        dst = wslab[r % 2][:, 0:512].rearrange("p (d c) -> p d c", d=4)
                    em.op("vector", "tensor_copy", dst, pb.rearrange("p (i c) -> p i c", i=4),
                          sem=g.sXP)
                    xp_round[(j, r)] = g.sXP.n
                    em.wait("tensor", g.sXP, xp_round[(j, r)])
                    if r == 0:
                        em.wait("tensor", g.sCast, wkv_ready)
                    for i in range(4):
                        dt = r * 4 + i
                        if own:
                            lx = xnT[:, dt * 1024 + owncol * P: dt * 1024 + (owncol + 1) * P]
                        else:
                            lx = wslab[r % 2][:, i * P:(i + 1) * P]
                        st, sp = (dt == 0), (dt == 15)
                        em.op("tensor", "matmul", psb(0, 512), lx, wkb[:, dt * 512:(dt + 1) * 512],
                              start=st, stop=sp, sem=(g.sTm if sp else None))
                        em.op("tensor", "matmul", psb(512, 512), lx, wvb[:, dt * 512:(dt + 1) * 512],
                              start=st, stop=sp, sem=(g.sTm if sp else None))
                kv_done = g.sTm.n

                em.wait("vector", g.sTm, kv_done)
                vdst = vall.rearrange("p (t h c) -> p t h c", t=NT, h=HKV)[:, j, :, 0:64]
                em.op("vector", "tensor_tensor", vdst,
                      psb(512, 512).rearrange("p (h c) -> p h c", h=HKV),
                      biasb[:, 512:1024].rearrange("p (h c) -> p h c", h=HKV),
                      ALU.add, sem=g.sKVe)
                em.wait("vector", g.sX, x_ready)
                em.op("vector", "tensor_tensor", qa, psb(0, 512), biasb[:, 0:512], ALU.add,
                      sem=g.sKVe)
                em.wait("scalar", g.sKVe, g.sKVe.n)
                v3 = qa.rearrange("p (h a c) -> p h a c", h=HKV, a=2)
                r3 = qb.rearrange("p (h a c) -> p h a c", h=HKV, a=2)
                em.op("scalar", "copy", r3[:, :, 0, :], v3[:, :, 1, :])
                em.op("scalar", "copy", r3[:, :, 1, :], v3[:, :, 0, :], sem=g.sLN)
                tabc = tab_[:, pj * 1024:pj * 1024 + 512]
                tabs = tab_[:, pj * 1024 + 512:pj * 1024 + 1024]
                em.op("vector", "tensor_tensor", qc, qa, tabc, ALU.mult)
                em.wait("vector", g.sLN, g.sLN.n)
                em.op("vector", "tensor_tensor", qd, qb, tabs, ALU.mult)
                em.op("vector", "tensor_tensor", kropeb, qc, qd, ALU.add, sem=g.sKVe)
                tabk_used[j] = g.sKVe.n
                em.wait("tensor", g.sKVe, tabk_used[j])
                pkb = psbf(2048, 256)
                for i in range(4):
                    em.op("tensor", "transpose", pkb[:, i * P:(i + 1) * P],
                          kropeb[:, i * P:(i + 1) * P], identb, sem=(g.sTm if i == 3 else None))
                em.wait("vector", g.sTm, g.sTm.n)
                kde = kaugT.rearrange("p (h c) -> p h c", h=HKV)
                em.op("vector", "tensor_copy", kde[0:64, 0::2, j * P:(j + 1) * P],
                      pkb[0:64, :].rearrange("p (i c) -> p i c", i=4))
                em.op("vector", "tensor_copy", kde[0:64, 1::2, j * P:(j + 1) * P],
                      pkb[64:128, :].rearrange("p (i c) -> p i c", i=4), sem=g.sKVe)
                em.op("vector", "tensor_tensor", sqb, kropeb, kropeb, ALU.mult)
                em.op("vector", "reduce_sum",
                      normk[:, j * 8:(j + 1) * 8].unsqueeze(2),
                      sqb.rearrange("p (h c) -> p h c", h=HKV), axis=AX.X, sem=g.sVs)
                em.wait("vector", g.sVs, g.sVs.n)
                em.op("vector", "tensor_tensor", kacc[:, :], kacc[:, :],
                      normk[:, j * 8:(j + 1) * 8], ALU.max, sem=g.sKVe)
                kvevac_done[j] = g.sKVe.n

            kv_all_done = g.sKVe.n
            kv_mm_done = g.sTm.n

            em.wait("vector", g.sKVe, g.sKVe.n)
            em.op("vector", "tensor_copy", lnst[:, 0:8], kacc[:, :], sem=g.sMean)
            em.wait("tensor", g.sMean, g.sMean.n)
            em.wait("tensor", g.sKVe, kv_all_done)
            em.op("tensor", "transpose", psb(2048, 128)[0:8, :], lnst[:, 0:8], identf[:, :], sem=g.sTm)
            em.wait("scalar", g.sTm, g.sTm.n)
            em.op("scalar", "copy", kmrow[0:8, :], psb(2048, 128)[0:8, 0:P], sem=g.sLN)
            em.wait("vector", g.sLN, g.sLN.n)
            em.op("vector", "reduce_max", km8[:, 0:1], kmrow[0:8, :], axis=AX.X, sem=g.sVs)
            em.wait("vector", g.sVs, g.sVs.n)
            em.op("vector", "tensor_scalar_mul", km8[:, :], km8[:, :], 1.0609, sem=g.sMean)
            em.wait("gpsimd", g.sMean, g.sMean.n)
            em.op("gpsimd", "dma_start", out=kmrow2[:, :], in_=km8[:, :], sem=g.sB, k=16)
            em.wait("vector", g.sB, g.sB.n)
            for jj in range(4):
                em.op("vector", "tensor_copy",
                      km32.rearrange("p (k j2) -> p k j2", j2=4)[:, :, jj], kmrow2[:, :],
                      sem=(g.sMean if jj == 3 else None))
            em.wait("tensor", g.sMean, g.sMean.n)
            em.op("tensor", "matmul", psb(2048, 32), onesf[0:1, :], km32[0:1, :],
                  start=True, stop=True, skip_group_check=True, sem=g.sTm)
            em.wait("scalar", g.sTm, g.sTm.n)
            em.op("scalar", "copy", kmax2b[:, :], psb(2048, 32), sem=g.sLN)
            kmax_ready = g.sLN.n

            em.wait("sync", g.sKVe, kv_all_done)
            em.op("sync", "dma_start", out=biasb[:, :],
                  in_=dbq.unsqueeze(0).partition_broadcast(P), sem=g.sX, k=16)
            bq_ready = g.sX.n

            prev_chunk_S_done = None
            prev_chunk_out_done = None
            att_pi = 0
            pair_e = {}
            yt_hist = {}
            av_full = {}

            for ci, ch in enumerate(chunks):
                b0 = ch * NB
                njt = b0 + NB

                qgrp_done = {}
                rope_done = {}
                if ci == 1:
                    em.wait("vector", g.sS, prev_chunk_S_done)
                    em.wait("gpsimd", g.sS, prev_chunk_S_done)
                    em.wait("vector", g.sOutW, prev_chunk_out_done)
                tr_done = {}
                evac_hist = []
                for ecg in range(2):
                    for dt in range(16):
                        sl = dt % 2
                        if (ecg * 16 + dt) >= 2:
                            key = (ecg, dt - 2) if dt >= 2 else (ecg - 1, 14 + dt)
                            em.wait("gpsimd", g.sTm, qgrp_done[key])
                        else:
                            em.wait("gpsimd", g.sTm, g.sTm.n)
                        em.op("gpsimd", "dma_start", out=wslab[sl],
                              in_=dwq[dt * P:(dt + 1) * P, ecg * 1024:(ecg + 1) * 1024],
                              sem=g.sW, k=16)
                        em.wait("tensor", g.sW, g.sW.n)
                        if dt == 0:
                            em.wait("tensor", g.sX, bq_ready)
                            if ecg == 1:
                                em.wait("tensor", g.sQT, g.sQT.n)
                            if ci == 0 and ecg == 0:
                                em.wait("tensor", g.sLN, kmax_ready)
                            if ci == 1 and ecg == 0:
                                em.wait("tensor", g.sRB, g.sRB.n)
                                em.wait("tensor", g.sYT, g.sYT.n)
                                em.wait("tensor", g.sOv, g.sOv.n)
                        for e2 in range(2):
                            for bi in range(4):
                                bank = e2 * 4 + bi
                                em.op("tensor", "matmul", psb(bank * 512, 512),
                                      xnT[:, dt * 1024 + (ci * 4 + bi) * P:
                                          dt * 1024 + (ci * 4 + bi) * P + P],
                                      wslab[sl][:, e2 * 512:(e2 + 1) * 512],
                                      start=(dt == 0), stop=(dt == 15),
                                      sem=(g.sTm if (e2 == 1 and bi == 3) else None))
                        qgrp_done[(ecg, dt)] = g.sTm.n
                    qmm = g.sTm.n
                    for bi in range(4):
                        blk = b0 + bi
                        if len(evac_hist) >= 2:
                            em.wait("sync", g.sRope, evac_hist[-2])
                        em.op("sync", "dma_start",
                              out=tab_[:, (bi % 2) * 1024:(bi % 2 + 1) * 1024],
                              in_=dtq[blk * P:(blk + 1) * P, :], sem=g.sX, k=16)
                        tabq_ready = g.sX.n
                        for e2 in range(2):
                            ec = ecg * 2 + e2
                            bank = e2 * 4 + bi
                            em.wait("vector", g.sTm, qmm)
                            em.op("vector", "tensor_tensor", qa, psb(bank * 512, 512),
                                  biasb[:, ec * 512:(ec + 1) * 512], ALU.add, sem=g.sRope)
                            em.wait("scalar", g.sRope, g.sRope.n)
                            v3 = qa.rearrange("p (h a c) -> p h a c", h=8, a=2)
                            r3 = qb.rearrange("p (h a c) -> p h a c", h=8, a=2)
                            em.op("scalar", "copy", r3[:, :, 0, :], v3[:, :, 1, :])
                            em.op("scalar", "copy", r3[:, :, 1, :], v3[:, :, 0, :], sem=g.sLN)
                            em.wait("vector", g.sX, tabq_ready)
                            tq_c = tab_[:, (bi % 2) * 1024:(bi % 2) * 1024 + 512]
                            tq_s = tab_[:, (bi % 2) * 1024 + 512:(bi % 2) * 1024 + 1024]
                            em.op("vector", "tensor_tensor", qc, qa, tq_c, ALU.mult)
                            em.wait("vector", g.sLN, g.sLN.n)
                            rslot = ropeb[:, (bank % 2) * 512:(bank % 2) * 512 + 512]
                            if len(evac_hist) >= 2:
                                em.wait("vector", g.sTm, tr_done[evac_hist[-2]])
                            em.op("vector", "tensor_tensor", qd, qb, tq_s, ALU.mult)
                            em.op("vector", "tensor_tensor", rslot, qc, qd, ALU.add, sem=g.sRope)
                            rope_ms = g.sRope.n
                            em.op("vector", "tensor_tensor", sqb, rslot, rslot, ALU.mult)
                            em.op("vector", "reduce_sum",
                                  normsq.rearrange("p (b c) -> p b c", b=4)[:, bi,
                                                                            ec * 8:(ec + 1) * 8].unsqueeze(2),
                                  sqb.rearrange("p (h c) -> p h c", h=8), axis=AX.X, sem=g.sRope)
                            em.wait("tensor", g.sRope, rope_ms)
                            ptq = psbf(bank * 512, 256)
                            for i in range(4):
                                em.op("tensor", "transpose", ptq[:, i * P:(i + 1) * P],
                                      rslot[:, i * P:(i + 1) * P], identb,
                                      sem=(g.sTm if i == 3 else None))
                            tr_done[rope_ms] = g.sTm.n
                            evac_hist.append(rope_ms)
                            em.wait("vector", g.sTm, g.sTm.n)
                            qde = qaugT.rearrange("p (gh c) -> p gh c", gh=HQ)
                            em.op("vector", "tensor_copy",
                                  qde[0:64, 8 * ec + 0:8 * ec + 8:2, bi * P:(bi + 1) * P],
                                  ptq[0:64, :].rearrange("p (i c) -> p i c", i=4))
                            em.op("vector", "tensor_copy",
                                  qde[0:64, 8 * ec + 1:8 * ec + 8:2, bi * P:(bi + 1) * P],
                                  ptq[64:128, :].rearrange("p (i c) -> p i c", i=4), sem=g.sQT)

                em.wait("vector", g.sLN, kmax_ready)
                em.wait("tensor", g.sQT, g.sQT.n)
                for bi in range(4):
                    nsq = normsq.rearrange("p (b c) -> p b c", b=4)[:, bi, 0:32]
                    em.op("vector", "tensor_tensor", qc[:, 0:32], nsq, kmax2b[:, :], ALU.mult,
                          sem=g.sMean)
                    em.wait("scalar", g.sMean, g.sMean.n)
                    em.op("scalar", "activation", qd[:, 0:32], qc[:, 0:32],
                          AF.Sqrt, sem=g.sLN)
                    em.wait("tensor", g.sLN, g.sLN.n)
                    em.wait("tensor", g.sMean, g.sMean.n)
                    em.op("tensor", "transpose", psb(2048, 128)[0:32, :], qd[:, 0:32], identf[:, :],
                          sem=g.sTm)
                    em.wait("vector", g.sTm, g.sTm.n)
                    em.wait("vector", g.sB, g.sB.n)
                    em.op("vector", "tensor_copy", bTb[0:32, :], psb(2048, 128)[0:32, 0:P],
                          sem=g.sMean)
                    em.wait("gpsimd", g.sMean, g.sMean.n)
                    em.op("gpsimd", "dma_start",
                          out=qaugT[64:65, :].rearrange("p (gh c) -> p gh c", gh=HQ)[:, :,
                                                                                     bi * P:(bi + 1) * P],
                          in_=bTb[0:32, :], sem=g.sB, k=16)

                if debug:
                    em.wait("gpsimd", g.sQT, g.sQT.n)
                    em.op("gpsimd", "dma_start", out=ddbg["qaug2"][ci], in_=qaugT, sem=g.sCast, k=16)
                em.wait("tensor", g.sQT, g.sQT.n)
                em.wait("tensor", g.sB, g.sB.n)
                em.wait("tensor", g.sKVe, kv_all_done)

                def epi_sc(pi):
                    p = pi % 2
                    avb = 2048 + p * 1024
                    em.wait("scalar", g.sAV, g.sAV.n)
                    if pi - 2 in yt_hist:
                        em.wait("scalar", g.sYT, yt_hist[pi - 2])
                    em.op("scalar", "activation", rbf2[0:1, p * 1024:p * 1024 + 1024],
                          ps[64:65, avb:avb + 1024], AF.Ln, sem=g.sRC)
                    em.wait("scalar", g.sRC, g.sRC.n)
                    em.op("scalar", "activation", rb[0:1, p * 1024:p * 1024 + 1024],
                          rbf2[0:1, p * 1024:p * 1024 + 1024], AF.Exp, scale=-1.0, sem=g.sRC)
                    em.wait("scalar", g.sRC, g.sRC.n)
                    em.op("scalar", "copy", rbb[0:1, p * 1024:p * 1024 + 1024],
                          rb[0:1, p * 1024:p * 1024 + 1024], sem=g.sRC)

                def epi_bcast(pi):
                    p = pi % 2
                    avb = 2048 + p * 1024
                    em.wait("tensor", g.sRC, g.sRC.n)
                    for c in range(2):
                        em.op("tensor", "matmul",
                              ps[64:128, avb + c * 512: avb + (c + 1) * 512],
                              onesb[0:1, 0:64],
                              rbb[0:1, p * 1024 + c * 512: p * 1024 + (c + 1) * 512],
                              start=True, stop=True, skip_group_check=True,
                              tile_position=(0, 64),
                              sem=(g.sBC if c == 1 else None))

                def epi_norm(pi, gg, h):
                    p = pi % 2
                    avb = 2048 + p * 1024
                    em.wait("scalar", g.sBC, g.sBC.n)
                    if pi - 2 in yt_hist:
                        em.wait("scalar", g.sYT, yt_hist[pi - 2])
                    em.op("scalar", "copy", rb[0:64, p * 1024:p * 1024 + 1024],
                          ps[64:128, avb:avb + 1024], sem=g.sRB)
                    em.wait("vector", g.sRB, g.sRB.n)
                    for c in range(2):
                        gq = gg * 4 + 2 * h + c
                        em.op("vector", "tensor_tensor",
                              yt[(gq % 2) * 64:(gq % 2) * 64 + 64,
                                 (gq // 2) * 512:(gq // 2) * 512 + 512],
                              ps[0:64, avb + c * 512: avb + (c + 1) * 512],
                              rb[0:64, p * 1024 + c * 512: p * 1024 + (c + 1) * 512],
                              ALU.mult, sem=(g.sYT if c == 1 else None))
                    yt_hist[pi] = g.sYT.n

                pend = None
                for gg in range(8):
                    kh = gg
                    for h in range(2):
                        pi = att_pi
                        p = pi % 2
                        avb = 2048 + p * 1024
                        av_pass = {}

                        def emit_av(jt, i, q0, m_cnt, e_cnt):
                            pe = jt % 2
                            if i >= 0:
                                em.wait("tensor", g.sM, m_cnt)
                            else:
                                em.wait("tensor", g.sE, e_cnt)
                            if jt == 0 and pi - 2 in yt_hist:
                                em.wait("tensor", g.sYT, yt_hist[pi - 2])
                            for c in range(2):
                                s4 = 2 * h + c
                                em.op("tensor", "matmul",
                                      ps[0:65, avb + c * 512 + q0: avb + (c + 1) * 512],
                                      vall[0:128, jt * 520 + kh * 65: jt * 520 + kh * 65 + 65],
                                      e4[pe][:, s4 * 512 + q0: (s4 + 1) * 512],
                                      start=(jt == 0), stop=(jt == njt - 1), skip_group_check=True,
                                      sem=(g.sAV if c == 1 else None))
                            av_pass[jt] = g.sAV.n

                        av_prev = None
                        for jt in range(njt):
                            i = jt - b0
                            q0 = max(0, i) * P
                            pe = jt % 2
                            sbase = pe * 1024
                            if pe in pair_e:
                                em.wait("tensor", g.sE, pair_e[pe])
                            for c in range(2):
                                gq = gg * 4 + 2 * h + c
                                em.op("tensor", "matmul",
                                      ps[0:128, sbase + c * 512 + q0: sbase + (c + 1) * 512],
                                      kaugT[0:65, kh * 2048 + jt * P: kh * 2048 + (jt + 1) * P],
                                      qaugT[0:65, gq * 512 + q0: (gq + 1) * 512],
                                      start=True, stop=True, skip_group_check=True,
                                      sem=(g.sS if c == 1 else None))
                            s_cnt = g.sS.n
                            if jt == 0 and pend is not None:
                                epi_bcast(pend[0])
                            em.wait("scalar", g.sS, s_cnt)
                            if jt >= 2:
                                em.wait("scalar", g.sAV, av_pass[jt - 2])
                            elif pi - 2 in av_full:
                                em.wait("scalar", g.sAV, av_full[pi - 2])
                            eview = e4[pe].rearrange("p (s c) -> p s c", s=4)[:, 2 * h:2 * h + 2, q0:512]
                            sview = ps[:, sbase:sbase + 1024].rearrange("p (s c) -> p s c", s=2)[:, :, q0:512]
                            em.op("scalar", "activation", eview, sview, AF.Exp, sem=g.sE)
                            e_cnt = g.sE.n
                            pair_e[pe] = e_cnt
                            m_cnt = 0
                            if i >= 0:
                                em.wait("vector", g.sE, e_cnt)
                                for c in range(2):
                                    s4 = 2 * h + c
                                    em.op("vector", "tensor_tensor",
                                          e4[pe][:, s4 * 512 + q0: s4 * 512 + q0 + P],
                                          e4[pe][:, s4 * 512 + q0: s4 * 512 + q0 + P],
                                          maskb, ALU.mult,
                                          sem=(g.sM if c == 1 else None))
                                m_cnt = g.sM.n
                            if jt == 0 and pend is not None:
                                epi_norm(*pend)
                                pend = None
                            if av_prev is not None:
                                emit_av(*av_prev)
                            av_prev = (jt, i, q0, m_cnt, e_cnt)
                        emit_av(*av_prev)
                        av_full[pi] = g.sAV.n
                        epi_sc(pi)
                        pend = (pi, gg, h)
                        att_pi += 1
                epi_bcast(pend[0])
                epi_norm(*pend)
                pend = None
                prev_chunk_S_done = g.sS.n

                em.wait("tensor", g.sYT, g.sYT.n)
                wogrp_done = {}
                ov_hist = []
                outw_hist = []
                for ecg in range(2):
                    for et in range(16):
                        sl = et % 2
                        if (ecg * 16 + et) >= 2:
                            key = (ecg, et - 2) if et >= 2 else (ecg - 1, 14 + et)
                            em.wait("gpsimd", g.sTm, wogrp_done[key])
                        else:
                            em.wait("gpsimd", g.sRB, g.sRB.n)
                            em.wait("gpsimd", g.sQT, g.sQT.n)
                        em.op("gpsimd", "dma_start", out=wslab[sl],
                              in_=dwo[et * P:(et + 1) * P, ecg * 1024:(ecg + 1) * 1024],
                              sem=g.sW, k=16)
                        em.wait("tensor", g.sW, g.sW.n)
                        if et == 0:
                            if ecg == 1:
                                em.wait("tensor", g.sOv, g.sOv.n)
                            else:
                                em.wait("tensor", g.sRB, g.sRB.n)
                                em.wait("tensor", g.sYT, g.sYT.n)
                        for e2 in range(2):
                            for bi in range(4):
                                bank = e2 * 4 + bi
                                em.op("tensor", "matmul", psb(bank * 512, 512),
                                      yt[:, et * 512 + bi * P: et * 512 + (bi + 1) * P],
                                      wslab[sl][:, e2 * 512:(e2 + 1) * 512],
                                      start=(et == 0), stop=False, skip_group_check=True,
                                      sem=(g.sTm if (e2 == 1 and bi == 3) else None))
                        wogrp_done[(ecg, et)] = g.sTm.n
                    for e2 in range(2):
                        ec = ecg * 2 + e2
                        for bi in range(4):
                            bank = e2 * 4 + bi
                            blk = b0 + bi
                            em.op("tensor", "matmul", psb(bank * 512, 512),
                                  onesb[0:1, :], bob2[0:1, ec * 512:(ec + 1) * 512],
                                  start=False, stop=True, skip_group_check=True, sem=g.sTm)
                            womm = g.sTm.n
                            inst = len(ov_hist)
                            if inst >= 4:
                                em.wait("sync", g.sOv, ov_hist[-4])
                            else:
                                em.wait("sync", g.sB, g.sB.n)
                            em.op("sync", "dma_start", out=xr4[inst % 4],
                                  in_=dx[blk * P:(blk + 1) * P, ec * 512:(ec + 1) * 512],
                                  sem=g.sX, k=16)
                            em.wait("vector", g.sX, g.sX.n)
                            em.wait("vector", g.sTm, womm)
                            if len(outw_hist) >= 2:
                                em.wait("vector", g.sOutW, outw_hist[-2])
                            em.op("vector", "tensor_tensor", qa if inst % 2 == 0 else qb,
                                  psb(bank * 512, 512), xr4[inst % 4], ALU.add, sem=g.sOv)
                            ov_hist.append(g.sOv.n)
                            em.wait("sync", g.sOv, g.sOv.n)
                            em.op("sync", "dma_start",
                                  out=dout[(ci * 4 + bi) * P:(ci * 4 + bi + 1) * P,
                                           ec * 512:(ec + 1) * 512],
                                  in_=qa if inst % 2 == 0 else qb, sem=g.sOutW, k=16)
                            outw_hist.append(g.sOutW.n)
                prev_chunk_out_done = g.sOutW.n

            all_attn_S = g.sS.n
            all_out_w = g.sOutW.n

            if debug:
                em.wait("gpsimd", g.sOutW, all_out_w)
                for _src, _dst in ((xnT, ddbg["xnT"]), (kaugT, ddbg["kaug"]), (vall, ddbg["vall"]),
                                   (qaugT, ddbg["qaug"]), (yt, ddbg["yt"])):
                    em.op("gpsimd", "dma_start", out=_dst, in_=_src, sem=g.sCast, k=16)
                em.op("gpsimd", "dma_start", out=ddbg["misc"][:, 0:32], in_=kmax2b[:, :], sem=g.sCast, k=16)
                em.op("gpsimd", "dma_start", out=ddbg["misc"][:, 32:160], in_=normsq[:, :], sem=g.sCast, k=16)
                em.wait("gpsimd", g.sCast, g.sCast.n)

            em.wait("gpsimd", g.sTm, g.sTm.n)
            em.wait("gpsimd", g.sS, all_attn_S)
            em.wait("vector", g.sAV, g.sAV.n)
            em.wait("vector", g.sS, all_attn_S)
            em.wait("vector", g.sOutW, all_out_w)
            em.wait("scalar", g.sOutW, all_out_w)
            em.wait("scalar", g.sS, all_attn_S)
            em.wait("scalar", g.sYT, g.sYT.n)
            ffn_grp_done = {}
            ga_hist = []
            de_hist = []
            accum_hist = []
            for hi in range(2):
                for hg in range(32):
                    sl = hg % 2
                    if (hi, hg - 2) in ffn_grp_done or (hi - 1, 30 + hg) in ffn_grp_done:
                        prevk = (hi, hg - 2) if hg >= 2 else (hi - 1, 30 + hg)
                        em.wait("gpsimd", g.sTm, ffn_grp_done[prevk])
                    em.op("gpsimd", "dma_start", out=wgs[sl], in_=dwg[hg],
                          sem=g.sW, k=16)
                    em.op("gpsimd", "dma_start", out=wus[sl], in_=dwu[hg],
                          sem=g.sW, k=16)
                    em.wait("tensor", g.sW, g.sW.n)
                    for h2 in range(2):
                        ht = hg * 2 + h2
                        gbk = (ht % 2) * 2
                        ubk = gbk + 1
                        if len(ga_hist) >= 2:
                            em.wait("tensor", g.sGa, ga_hist[-2])
                        elif hi == 0 and hg == 0 and h2 == 0:
                            em.wait("tensor", g.sOv, g.sOv.n)
                        for dt in range(16):
                            em.op("tensor", "matmul", psb(gbk * 512, 512),
                                  wgs[sl][:, dt * 256 + h2 * P: dt * 256 + (h2 + 1) * P],
                                  xnT[:, dt * 1024 + hi * 512: dt * 1024 + (hi + 1) * 512],
                                  start=(dt == 0), stop=(dt == 15),
                                  sem=(g.sTm if dt == 15 else None))
                        for dt in range(16):
                            em.op("tensor", "matmul", psb(ubk * 512, 512),
                                  wus[sl][:, dt * 256 + h2 * P: dt * 256 + (h2 + 1) * P],
                                  xnT[:, dt * 1024 + hi * 512: dt * 1024 + (hi + 1) * 512],
                                  start=(dt == 0), stop=(dt == 15),
                                  sem=(g.sTm if dt == 15 else None))
                        em.wait("scalar", g.sTm, g.sTm.n)
                        em.op("scalar", "activation", qd, psb(gbk * 512, 512), AF.Silu, sem=g.sSi)
                        em.wait("vector", g.sSi, g.sSi.n)
                        em.op("vector", "tensor_tensor", gated[:, ht * 512:(ht + 1) * 512],
                              qd, psb(ubk * 512, 512), ALU.mult, sem=g.sGa)
                        ga_hist.append(g.sGa.n)
                    ffn_grp_done[(hi, hg)] = g.sTm.n
                em.wait("tensor", g.sGa, g.sGa.n)
                wd_done = {}
                for ec in range(4):
                    for hgrp in range(8):
                        sl = hgrp % 2
                        if (ec, hgrp - 2) in wd_done or (ec - 1, 6 + hgrp) in wd_done:
                            prevk = (ec, hgrp - 2) if hgrp >= 2 else (ec - 1, 6 + hgrp)
                            em.wait("gpsimd", g.sTm, wd_done[prevk])
                        em.op("gpsimd", "dma_start", out=wds[sl], in_=dwd[ec, hgrp],
                              sem=g.sW, k=16)
                        em.wait("tensor", g.sW, g.sW.n)
                        if hgrp == 0 and len(de_hist) >= 4:
                            em.wait("tensor", g.sDe, de_hist[-4])
                        for hh in range(8):
                            ht = hgrp * 8 + hh
                            for bi in range(4):
                                em.op("tensor", "matmul", psb((4 + bi) * 512, 512),
                                      gated[:, ht * 512 + bi * P: ht * 512 + (bi + 1) * P],
                                      wds[sl][:, hh * 512:(hh + 1) * 512],
                                      start=(ht == 0), stop=(ht == 63), skip_group_check=True,
                                      sem=(g.sTm if (hh == 7 and bi == 3) else None))
                        wd_done[(ec, hgrp)] = g.sTm.n
                    em.wait("scalar", g.sTm, g.sTm.n)
                    for bi in range(4):
                        inst = len(de_hist)
                        if inst >= 2:
                            em.wait("scalar", g.sOutF, accum_hist[-2])
                        em.op("scalar", "copy", qa if inst % 2 == 0 else qb,
                              psb((4 + bi) * 512, 512), sem=g.sDe)
                        de_hist.append(g.sDe.n)
                        em.wait("gpsimd", g.sDe, g.sDe.n)
                        if len(accum_hist) == 0:
                            em.wait("gpsimd", g.sOutW, all_out_w)
                        if ffn_split:
                            em.op("gpsimd", "dma_start",
                                  out=dffn[(hi * 4 + bi) * P:(hi * 4 + bi + 1) * P,
                                           ec * 512:(ec + 1) * 512],
                                  in_=qa if inst % 2 == 0 else qb,
                                  sem=g.sOutF, k=16)
                        else:
                            em.op("gpsimd", "dma_start",
                                  out=dout[(hi * 4 + bi) * P:(hi * 4 + bi + 1) * P,
                                           ec * 512:(ec + 1) * 512],
                                  in_=qa if inst % 2 == 0 else qb, accum_op=ALU.add,
                                  sem=g.sOutF, k=16)
                        accum_hist.append(g.sOutF.n)

            em.wait("gpsimd", g.sOutF, g.sOutF.n)
            em.wait("gpsimd", g.sOutW, g.sOutW.n)
            em.wait("sync", g.sOutF, g.sOutF.n)
            em.wait("sync", g.sOutW, g.sOutW.n)

        def emit_engine(name):
            eng = getattr(nc, name)
            for fn in em.pre[name]:
                fn()
            eng.wait_ge(sems["sFlag"].h, 16)
            with eng.register("r_" + name) as r:
                eng.reg_load(r, fl[0:1, 0:1])
                with eng.If_eq(r, 0):
                    for fn in em.bod[0][name]:
                        fn()
                with eng.Else():
                    for fn in em.bod[1][name]:
                        fn()

        @block.sync
        def _(s):
            emit_engine("sync")

        @block.tensor
        def _(t):
            emit_engine("tensor")

        @block.vector
        def _(v):
            emit_engine("vector")

        @block.scalar
        def _(s):
            emit_engine("scalar")

        @block.gpsimd
        def _(gp):
            emit_engine("gpsimd")

    return nc


def _host_inputs(inputs):
    x = np.asarray(inputs["L0_attn_x"], np.float32)
    wq = np.ascontiguousarray(np.asarray(inputs["L0_attn_Wq"], np.float32))
    wk = np.ascontiguousarray(np.asarray(inputs["L0_attn_Wk"], np.float32))
    wv = np.ascontiguousarray(np.asarray(inputs["L0_attn_Wv"], np.float32))
    wo = np.ascontiguousarray(np.asarray(inputs["L0_attn_Wo"], np.float32))
    bq = np.ascontiguousarray(np.asarray(inputs["L0_attn_bq"], np.float32))
    bk = np.ascontiguousarray(np.asarray(inputs["L0_attn_bk"], np.float32))
    bv = np.ascontiguousarray(np.asarray(inputs["L0_attn_bv"], np.float32))
    bo = np.ascontiguousarray(np.asarray(inputs["L0_attn_bo"], np.float32))
    wg = np.asarray(inputs["L0_ffn_w_gate"], np.float32)
    wu = np.asarray(inputs["L0_ffn_w_up"], np.float32)
    wd = np.asarray(inputs["L0_ffn_w_down"], np.float32)

    wg2 = np.ascontiguousarray(wg.reshape(32, 256, 16, 128).transpose(0, 3, 2, 1).reshape(32, 128, 4096))
    wu2 = np.ascontiguousarray(wu.reshape(32, 256, 16, 128).transpose(0, 3, 2, 1).reshape(32, 128, 4096))
    wd4 = np.ascontiguousarray(
        wd.reshape(4, 512, 8, 8, 128).transpose(0, 2, 4, 3, 1).reshape(4, 8, 128, 4096))

    pos = np.arange(1, T + 1, dtype=np.float64)[:, None]
    freqs = 10000.0 ** (-(2.0 * np.arange(D // 2, dtype=np.float64)) / D)
    th = pos * freqs
    cos = np.cos(th)
    sin = np.sin(th)
    Ct = np.concatenate([cos, cos], 1)
    St = np.concatenate([-sin, sin], 1)
    Crep = np.tile(Ct, (1, 8))
    Srep = np.tile(St, (1, 8))
    tk = np.ascontiguousarray(np.concatenate([Crep, Srep], 1).astype(np.float32))
    tq = np.ascontiguousarray((np.concatenate([Crep, Srep], 1) * 0.125).astype(np.float32))
    identm = np.eye(P, dtype=np.float32)
    maskm = np.ascontiguousarray(np.triu(np.ones((P, P), np.float32)))

    in_maps = []
    for c in range(8):
        b, par = c // 2, c % 2
        in_maps.append({
            "x": np.ascontiguousarray(x[b]),
            "wq": wq, "wk": wk, "wv": wv, "wo": wo,
            "bq": bq, "bk": bk, "bv": bv, "bo": bo,
            "wg2": wg2, "wu2": wu2, "wd4": wd4,
            "tq": tq, "tk": tk, "identm": identm, "maskm": maskm,
            "flag": np.array([par], np.int32),
        })
    return in_maps


_NC_CACHE = {}


def kernel(**inputs):
    in_maps = _host_inputs(inputs)
    if "nc" not in _NC_CACHE:
        _NC_CACHE["nc"] = build_nc(debug=False)
    nc = _NC_CACHE["nc"]
    res = run_bass_kernel_spmd(nc, in_maps, core_ids=list(range(8)))
    out = np.zeros((4, T, E), np.float32)
    for c in range(8):
        b, par = c // 2, c % 2
        o = np.asarray(res.results[c]["out"], np.float32)
        lo, hi = CHUNKS[par]
        out[b, lo * 512:(lo + 1) * 512] = o[0:512]
        out[b, hi * 512:(hi + 1) * 512] = o[512:1024]
    return out
